# version 27
# baseline (speedup 1.0000x reference)
import contextlib

import numpy as np
import ml_dtypes

import concourse.bass as bass
import concourse.mybir as mybir
from concourse.bass_utils import run_bass_kernel_spmd

F32 = mybir.dt.float32
F32R = mybir.dt.float32r
BF16 = mybir.dt.bfloat16
F8 = mybir.dt.float8e4
ACT = mybir.ActivationFunctionType
ALU = mybir.AluOpType
AX = mybir.AxisListType.X

D = 1024
H = 16
HD = 64
C = 8
TT = 256
NCORES = 8

SM_MG, SM_MR, SM_MK, SM_MV = 0, 8, 16, 24
SM_BG, SM_BR, SM_BK, SM_BV = 32, 40, 48, 56
SM_W, SM_U = 64, 72
SM_WS = 80
SM_ONE = 81
SM_MG1, SM_MR1, SM_MK1, SM_MV1 = 88, 96, 104, 112
SM_COLS = 120

NOPS = 220
DEBUG_NOLUT = False
DEBUG_NOSCAN = False
STOP_LEVEL = 9


class Sched:

    def __init__(self, nc):
        self.nc = nc
        self.ops = []
        self.sems = {}
        self.counts = {}

    def sem(self, name):
        if name not in self.sems:
            self.sems[name] = self.nc.alloc_semaphore(name=name)
            self.counts[name] = 0
        return self.sems[name]

    def emit(self, eng, fn, waits=(), inc=None, inc_n=1):
        best = {}
        for x in waits:
            if x is None or x[1] <= 0:
                continue
            if x[0] not in best or x[1] > best[x[0]]:
                best[x[0]] = x[1]
        w = list(best.items())
        tok = None
        if inc is not None:
            self.sem(inc)
            if inc_n == 16:
                prev = self.counts[inc]
                if prev > 0 and not best.get(inc, 0) >= prev:
                    w.append((inc, prev))
            self.counts[inc] += inc_n
            tok = (inc, self.counts[inc])
        self.ops.append((eng, fn, w, self.sems.get(inc), inc_n if inc else 0))
        return tok


class Bank:

    def __init__(self, name):
        self.name = name
        self.free_tok = None

    def producer_wait(self):
        return self.free_tok

    def consumed(self, tok):
        self.free_tok = tok


def build(NT=32, taps=(), tap_tile=0):
    N = NT * TT
    nc = bass.Bass("TRN2", target_bir_lowering=False, debug=False)

    x_h = nc.dram_tensor("x", [N, D], F32, kind="ExternalInput").ap()
    wT_h = {k: nc.dram_tensor(f"w{k}T", [D, D], F32, kind="ExternalInput").ap()
            for k in "grkvo"}
    w8T_h = nc.dram_tensor("w8T", [D, D], F32, kind="ExternalInput").ap()
    sm_h = nc.dram_tensor("smalls", [128, SM_COLS], F32, kind="ExternalInput").ap()
    pack2_h = nc.dram_tensor("pack2", [1, 2048], F32, kind="ExternalInput").ap()
    identf_h = nc.dram_tensor("identf", [128, 128], F32, kind="ExternalInput").ap()
    identb_h = nc.dram_tensor("identb", [128, 128], BF16, kind="ExternalInput").ap()
    bones_h = nc.dram_tensor("bones", [128, 128], BF16, kind="ExternalInput").ap()
    sel_h = nc.dram_tensor("sel", [16, 1024], BF16, kind="ExternalInput").ap()
    xlast_h = nc.dram_tensor("xlast", [1, D], F32, kind="ExternalInput").ap()
    out_h = nc.dram_tensor("out", [N, D], F32, kind="ExternalOutput").ap()
    tap_h = {}
    tap_specs = {
        "xn_tm": ([128, 2, D], F32), "qs_tm": ([128, 2, D], BF16),
        "xn_f": ([128, C, TT + 1], BF16), "q_f": ([128, C, TT], BF16),
        "mixA": ([128, C, TT], BF16), "mixB": ([128, C, TT], BF16),
        "kv": ([128, C, TT], F32), "acc": ([128, C, TT], F32),
        "rw": ([128, C, TT], F32), "gat": ([128, C, TT], BF16),
        "sgb": ([128, C, TT], BF16), "hbuf": ([16, 1024], F32),
        "stats": ([128, 64], F32), "bias_ops": ([2, 2048], BF16),
        "rbuf_r": ([128, C, TT], F32), "kbuf_k": ([128, C, TT], F32),
        "decr": ([128, C, TT], F32), "w8sb": ([128, C, D], BF16),
    }
    for t in taps:
        tap_h[t] = nc.dram_tensor("tap_" + t, tap_specs[t][0], tap_specs[t][1],
                                  kind="ExternalOutput").ap()

    sch = Sched(nc)
    for name in ["go", "cx", "cw", "cw1", "cw2", "ct", "cv", "ca", "co", "cg", "g2", "g3", "g4"]:
        sch.sem(name)

    es = contextlib.ExitStack()
    with es:
        sb = lambda name, shape, dt: es.enter_context(nc.sbuf_tensor(name, shape, dt))
        ps = lambda name, shape, dt: es.enter_context(nc.psum_tensor(name, shape, dt))

        W = {k: sb(f"W{k}", [128, C, D], BF16) for k in "grkvo"}
        W8 = sb("W8", [128, C, D], BF16)
        sm = sb("sm", [128, SM_COLS], F32)
        x_tm = [sb(f"x{i}", [128, 2, D], F32) for i in range(2)]
        xn_tm = sb("xn", [128, 2, D], F32)
        qs_tm = sb("qs", [128, 2, D], BF16)
        xn_f = [sb(f"xnf{i}", [128, C, TT], BF16) for i in range(2)]
        xcar = [sb(f"xcar{i}", [128, C, 1], F32) for i in range(2)]
        q_f = [sb(f"qf{i}", [128, C, TT], BF16) for i in range(2)]
        dxn = sb("dxn", [128, C, TT], BF16)
        mixA = sb("mixA", [128, C, TT], BF16)
        mixB = sb("mixB", [128, C, TT], BF16)
        sgb = sb("sgb", [128, C, TT], BF16)
        rbuf = sb("rbuf", [128, C, TT], BF16)
        kbuf = sb("kbuf", [128, C, TT], F32)
        accb = sb("accb", [128, C, TT], F32)
        carry = sb("carry", [128, C, 1], F32)
        gat = sb("gat", [128, C, TT], BF16)
        tmp = sb("tmp", [128, 2, 512], F32)
        outb = sb("outb", [128, 2, D], F32)
        stats = sb("stats", [128, 64], F32)
        hbuf = sb("hbuf", [16, 1024], F32)
        bias_ops = sb("bias_ops", [2, 2048], BF16)
        ones2 = sb("ones2", [2, 128], BF16)
        decv = sb("decv", [128, C], F32)
        ibln = sb("ibln", [128, 4], mybir.dt.int32)
        ibh = sb("ibh", [16, 256], mybir.dt.int32)
        identf = sb("identf_s", [128, 128], F32)
        identb = sb("identb_s", [128, 128], BF16)
        bones = sb("bones_s", [128, 128], BF16)
        sel = sb("sel_s", [16, 1024], BF16)
        lst = sb("lst", [1, 16], F32)
        q8_tm = (dxn[:, 0:4, :].rearrange("p c t -> p (c t)").bitcast(F8)
                 .rearrange("p (h d) -> p h d", d=D))
        rb_flat = rbuf[:, :, :].rearrange("p c t -> p (c t)")
        acc_flat = accb[:, :, :].rearrange("p c t -> p (c t)")
        p2row = acc_flat[0:1, 0:2048]
        xlast = outb[0:1, 0, :]
        xlast2 = outb[0:1, 1, :]

        pXT = ps("pXT", [128, 512], F32)
        pQT = ps("pQT", [128, 1024], BF16)
        pP = [ps("pP1", [128, 512], F32), ps("pP2", [128, 512], F32)]
        pHS = ps("pHS", [16, 512], F32)
        pHB = ps("pHB", [128, 512], F32)
        pW = [ps("pW1", [128, 512], F32), ps("pW2", [128, 512], F32)]

        bk_P = [Bank("pP1"), Bank("pP2")]
        bk_XT = Bank("pXT")
        bk_QT = Bank("pQT")
        bk_HS = Bank("pHS")
        bk_HB = Bank("pHB")
        bk_W = [Bank("pW1"), Bank("pW2")]

        E = sch.emit
        NF1 = np.float32(1.0)
        NF0 = np.float32(0.0)
        MAGIC = 0x5F3759DF

        def act(e, out, in_, func, scale=None, bias=None, accum_out=None):
            if func == ACT.Copy:
                func = ACT.Identity
            if DEBUG_NOLUT and func in (ACT.Sqrt, ACT.Sigmoid, ACT.Silu, ACT.Exp):
                func = ACT.Identity
            return e.activation(out=out, in_=in_, func=func,
                                scale=NF1 if scale is None else scale,
                                bias=NF0 if bias is None else bias,
                                accum_out=accum_out)

        def f32r(ap):
            return ap

        def clears(eng):
            for s in range(155, 220):
                eng.sem_clear(range(s, s + 1))
            return None
        E("sync", clears)
        E("sync", lambda e: e.sem_inc(sch.sems["go"], 1))
        go = ("go", 1)

        def dma(dst, src):
            return lambda e: e.dma_start(out=dst, in_=src)

        t_sm = E("sync", dma(sm[:, :], sm_h), inc="cx", inc_n=16)
        t_p2 = E("sync", dma(p2row, pack2_h), inc="cx", inc_n=16)
        t_idf = E("sync", dma(identf[:, :], identf_h), inc="cx", inc_n=16)
        t_idb = E("sync", dma(identb[:, :], identb_h), inc="cx", inc_n=16)
        t_bo = E("sync", dma(bones[:, :], bones_h), inc="cx", inc_n=16)
        t_sel = E("sync", dma(sel[:, :], sel_h), inc="cx", inc_n=16)
        t_xl = E("sync", dma(xlast, xlast_h), inc="cx", inc_n=16)
        t_consts = t_xl

        t_x = [None] * NT
        resid_done = [None] * NT

        def emit_xload(j):
            src = x_h[j * TT:(j + 1) * TT, :].rearrange("(h p) d -> p h d", p=128)
            t_x[j] = E("sync", dma(x_tm[j % 2][:, :, :], src),
                       waits=[resid_done[j - 2] if j >= 2 else None],
                       inc="cx", inc_n=16)

        emit_xload(0)
        emit_xload(1)

        def delay(eng):
            for _ in range(400):
                eng.nop()
            return None
        E("gpsimd", delay)
        E("gpsimd", lambda e: None, waits=[go])
        go2 = E("gpsimd", lambda e: e.nop(), inc="g2")

        def gwait(e):
            return None
        t_w = {}
        wch = {"g": "cw", "r": "cw1", "k": "cw2", "v": "cw", "o": "cw1"}
        for k in "grkvo":
            t_w[k] = E("gpsimd", dma(W[k][:, :, :],
                                     wT_h[k].rearrange("(c p) o -> p c o", p=128)),
                       inc=wch[k], inc_n=16)
        t_w8cast = E("gpsimd", dma(W8[:, :, :],
                                   w8T_h.rearrange("(c p) o -> p c o", p=128)),
                     inc="cw2", inc_n=16)

        E("vector", delay)
        E("vector", lambda e: None, waits=[go, go2])
        go3 = E("vector", lambda e: e.nop(), inc="g3")
        E("scalar", delay)
        E("scalar", lambda e: None, waits=[go, go3])
        go4 = E("scalar", lambda e: e.nop(), inc="g4")
        E("tensor", delay)
        E("tensor", lambda e: None, waits=[go, go4])


        def mneg(e):
            for base_src, base_dst in [(SM_MG, SM_MG1), (SM_MR, SM_MR1),
                                       (SM_MK, SM_MK1), (SM_MV, SM_MV1)]:
                for c in range(C):
                    e.tensor_scalar(out=sm[:, base_dst + c:base_dst + c + 1],
                                    in0=sm[:, base_src + c:base_src + c + 1],
                                    scalar1=1.0, scalar2=None, op0=ALU.subtract)
            return e.drain()
        t_mneg = E("vector", mneg, waits=[t_sm], inc="cv")

        t_msets = E("vector", lambda e: (
            e.memset(carry[:, :, :], 0.0),
            e.memset(ones2[:, :], 1.0))[-1], inc="cv")
        E("scalar", lambda e: act(e, lst[0:1, 15:16], sm[0:1, 0:1], ACT.Sigmoid),
          waits=[t_sm])

        t_z1 = E("scalar",
                 lambda e: act(e, decv[:, :], sm[:, SM_W:SM_W + 8], ACT.Tanh,
                               scale=np.float32(0.5)),
                 waits=[t_sm], inc="ca")

        def neg_e1(e):
            for c in range(C):
                e.tensor_scalar(out=stats[:, 56 + c:57 + c], in0=decv[:, c:c + 1],
                                scalar1=1.0, scalar2=None, op0=ALU.add)
            e.drain()
            for c in range(C):
                e.tensor_scalar(out=decv[:, c:c + 1], in0=decv[:, c:c + 1],
                                scalar1=1.0, scalar2=None, op0=ALU.subtract)
            e.drain()
            for c in range(C):
                e.reciprocal(decv[:, c:c + 1], decv[:, c:c + 1])
            e.drain()
            for c in range(C):
                e.tensor_scalar(out=decv[:, c:c + 1], in0=decv[:, c:c + 1],
                                scalar1=stats[:, 56 + c:57 + c], scalar2=None,
                                op0=ALU.mult)
            return e.drain()
        t_ne1 = E("vector", neg_e1, waits=[t_z1], inc="cv")
        t_z2 = E("scalar",
                 lambda e: act(e, decv[:, :], decv[:, :], ACT.Tanh,
                               scale=np.float32(0.5)),
                 waits=[t_ne1], inc="ca")

        def decay_fin(e):
            for c in range(C):
                e.tensor_scalar(out=stats[:, 56 + c:57 + c], in0=decv[:, c:c + 1],
                                scalar1=1.0, scalar2=None, op0=ALU.add)
            e.drain()
            for c in range(C):
                e.tensor_scalar(out=decv[:, c:c + 1], in0=decv[:, c:c + 1],
                                scalar1=1.0, scalar2=None, op0=ALU.subtract)
            e.drain()
            for c in range(C):
                e.reciprocal(decv[:, c:c + 1], decv[:, c:c + 1])
            e.drain()
            for c in range(C):
                e.tensor_scalar(out=decv[:, c:c + 1], in0=decv[:, c:c + 1],
                                scalar1=stats[:, 56 + c:57 + c], scalar2=None,
                                op0=ALU.mult)
            e.drain()
            for c in range(C):
                e.tensor_scalar(out=decv[:, c:c + 1], in0=decv[:, c:c + 1],
                                scalar1=-1.0, scalar2=None, op0=ALU.mult)
            return e.drain()
        t_dec = E("vector", decay_fin, waits=[t_z2], inc="cv")
        t_bhi = E("scalar",
                  lambda e: act(e, bias_ops[0:1, :], p2row, ACT.Copy),
                  waits=[t_p2], inc="ca")
        blo_scr = acc_flat[0:1, 0:1024].bitcast(BF16)
        t_blo_s = E("vector",
                    lambda e: e.tensor_tensor(out=blo_scr, in0=p2row,
                                              in1=bias_ops[0:1, :], op=ALU.subtract),
                    waits=[t_bhi], inc="cv")
        t_blo = E("sync", lambda e: e.dma_start(out=bias_ops[1:2, :], in_=blo_scr),
                  waits=[t_blo_s], inc="cx", inc_n=16)

        t_lsq = E("scalar",
                  lambda e: act(e, xn_tm[0:1, 0, :], xlast, ACT.Square,
                                accum_out=lst[:, 2:3]),
                  waits=[t_xl], inc="ca")
        t_lsum = E("vector",
                   lambda e: e.tensor_reduce(out=lst[:, 0:1], in_=xlast,
                                             axis=AX, op=ALU.add, negate=True),
                   waits=[t_xl], inc="cv")
        def lvarchain(e):
            e.tensor_scalar(out=lst[:, 4:5], in0=lst[:, 0:1], scalar1=1.0 / D,
                            scalar2=None, op0=ALU.mult)
            e.tensor_scalar(out=lst[:, 8:9], in0=lst[:, 2:3], scalar1=1.0 / D,
                            scalar2=None, op0=ALU.mult)
            e.drain()
            e.tensor_scalar(out=lst[:, 6:7], in0=lst[:, 4:5],
                            scalar1=lst[0:1, 4:5], scalar2=None, op0=ALU.mult)
            e.drain()
            return e.tensor_scalar(out=lst[:, 8:9], in0=lst[:, 8:9],
                                   scalar1=lst[0:1, 6:7], scalar2=None,
                                   op0=ALU.subtract)
        t_lvar = E("vector", lvarchain, waits=[t_lsq], inc="cv")
        def lrschain(e):
            e.tensor_scalar(out=lst[:, 10:11], in0=lst[:, 8:9], scalar1=1e-5,
                            scalar2=None, op0=ALU.add)
            e.drain()
            e.tensor_scalar(out=ibln[0:1, 2:3],
                            in0=lst[:, 10:11].bitcast(mybir.dt.int32),
                            scalar1=1, scalar2=None, op0=ALU.logical_shift_right)
            e.drain()
            e.tensor_scalar(out=ibln[0:1, 2:3], in0=ibln[0:1, 2:3],
                            scalar1=-1, scalar2=None, op0=ALU.bitwise_xor)
            e.drain()
            e.tensor_scalar(out=ibln[0:1, 2:3], in0=ibln[0:1, 2:3],
                            scalar1=MAGIC + 1, scalar2=None, op0=ALU.add)
            e.drain()
            e.tensor_copy(lst[:, 12:13], ibln[0:1, 2:3].bitcast(F32))
            e.drain()
            for _ in range(4):
                e.scalar_tensor_tensor(out=lst[:, 6:7], in0=lst[:, 12:13],
                                       scalar=lst[0:1, 10:11], in1=lst[:, 12:13],
                                       op0=ALU.mult, op1=ALU.mult)
                e.drain()
                e.tensor_scalar(out=lst[:, 6:7], in0=lst[:, 6:7], scalar1=-0.5,
                                scalar2=None, op0=ALU.mult)
                e.drain()
                e.tensor_scalar(out=lst[:, 6:7], in0=lst[:, 6:7], scalar1=1.5,
                                scalar2=None, op0=ALU.add)
                e.drain()
                e.tensor_scalar(out=lst[:, 12:13], in0=lst[:, 12:13],
                                scalar1=lst[0:1, 6:7], scalar2=None, op0=ALU.mult)
                e.drain()
            return e.tensor_scalar(out=lst[:, 14:15], in0=lst[:, 4:5],
                                   scalar1=lst[0:1, 12:13], scalar2=None,
                                   op0=ALU.mult)
        t_lrs = E("vector", lrschain, waits=[t_lvar, t_msets], inc="cv")
        t_lxn = E("scalar",
                  lambda e: act(e, xlast2, xlast, ACT.Identity,
                                scale=lst[0:1, 12:13], bias=lst[0:1, 14:15]),
                  waits=[t_lrs], inc="ca")
        t_lcol = None
        for c in range(C):
            t_lcol = E("tensor",
                       lambda e, c=c: e.matmul(pXT[:, c:c + 1],
                                               f32r(xlast2[0:1, 128 * c:128 * (c + 1)]),
                                               f32r(sm[0:1, SM_ONE:SM_ONE + 1]),
                                               start=True, stop=True),
                       waits=[t_lxn, t_sm] if c == 0 else (),
                       inc="ct" if c == C - 1 else None)
        t_xcol = E("vector",
                   lambda e: e.tensor_copy(xcar[0][:, :, 0:1], pXT[:, 0:8]),
                   waits=[t_lcol], inc="cv")
        bk_XT.consumed(t_xcol)

        tok_qs = [None] * (NT + 1)
        tok_sqmm = [None] * (NT + 1)
        tok_xnT = [None] * (NT + 1)
        tok_qT = [None] * (NT + 1)
        tok_gated = [None] * (NT + 1)
        tok_store = [None] * (NT + 1)
        tok_evq = [None] * (NT + 1)
        tok_evxn = [None] * (NT + 1)
        tok_ffn = [None] * (NT + 1)
        tok_rstd = [None] * (NT + 1)
        tok_mix = [None] * (NT + 1)
        slot_resid_all = [[None] * 4 for _ in range(NT + 1)]
        tok_wo = [None] * (NT + 1)
        prev_proj_done = {}

        tok_ssq = [None] * (NT + 1)
        tok_xn = [None] * (NT + 1)

        def emit_LN_A(j):
            if tok_ssq[j] is not None:
                return
            par = j % 2
            sc = 32 * par
            cur_x = x_tm[j % 2]
            w_sq = [t_x[j], tok_qT[j - 1] if j >= 1 else None]
            for h in range(2):
                E("scalar", lambda e, b=cur_x, h=h, s=sc: act(
                    e, qs_tm[:, h, :], b[:, h, :],
                    ACT.Square, accum_out=stats[:, s + 2 + h:s + 3 + h]),
                    waits=w_sq if h == 0 else ())
            tok_ssq[j] = E("scalar", lambda e: e.drain(), inc="ca")

        def emit_LN(j):
            par = j % 2
            sc = 32 * par
            cur_x = x_tm[j % 2]
            w_ln = [t_x[j], tok_qs[j - 2] if j >= 2 else None]
            emit_LN_A(j)
            t_nsum = None
            for h in range(2):
                t_nsum = E("vector", lambda e, b=cur_x, s=sc, h=h: e.tensor_reduce(
                    out=stats[:, s + h:s + 1 + h], in_=b[:, h, :], axis=AX,
                    op=ALU.add, negate=True), waits=w_ln if h == 0 else (),
                    inc="cv" if h == 1 else None)
            t_ssq = tok_ssq[j]

            def varchain(e, s=sc):
                e.drain()
                for h in range(2):
                    e.tensor_scalar(out=stats[:, s + 4 + h:s + 5 + h],
                                    in0=stats[:, s + 0 + h:s + 1 + h],
                                    scalar1=1.0 / D, scalar2=None, op0=ALU.mult)
                for h in range(2):
                    e.tensor_scalar(out=stats[:, s + 8 + h:s + 9 + h],
                                    in0=stats[:, s + 2 + h:s + 3 + h],
                                    scalar1=1.0 / D, scalar2=None, op0=ALU.mult)
                e.drain()
                for h in range(2):
                    e.tensor_scalar(out=stats[:, s + 6 + h:s + 7 + h],
                                    in0=stats[:, s + 4 + h:s + 5 + h],
                                    scalar1=stats[:, s + 4 + h:s + 5 + h],
                                    scalar2=None, op0=ALU.mult)
                e.drain()
                last = None
                for h in range(2):
                    last = e.tensor_scalar(out=stats[:, s + 8 + h:s + 9 + h],
                                           in0=stats[:, s + 8 + h:s + 9 + h],
                                           scalar1=stats[:, s + 6 + h:s + 7 + h],
                                           scalar2=None, op0=ALU.subtract)
                return last
            t_var = E("vector", varchain, waits=[t_ssq, t_nsum], inc="cv")

            def rstdchain(e, s=sc):
                for h in range(2):
                    e.tensor_scalar(out=stats[:, s + 10 + h:s + 11 + h],
                                    in0=stats[:, s + 8 + h:s + 9 + h],
                                    scalar1=1e-5, scalar2=None, op0=ALU.add)
                e.drain()
                for h in range(2):
                    e.tensor_scalar(out=ibln[:, h:h + 1],
                                    in0=stats[:, s + 10 + h:s + 11 + h].bitcast(
                                        mybir.dt.int32),
                                    scalar1=1, scalar2=None,
                                    op0=ALU.logical_shift_right)
                e.drain()
                for h in range(2):
                    e.tensor_scalar(out=ibln[:, h:h + 1], in0=ibln[:, h:h + 1],
                                    scalar1=-1, scalar2=None,
                                    op0=ALU.bitwise_xor)
                e.drain()
                for h in range(2):
                    e.tensor_scalar(out=ibln[:, h:h + 1], in0=ibln[:, h:h + 1],
                                    scalar1=MAGIC + 1, scalar2=None, op0=ALU.add)
                e.drain()
                for h in range(2):
                    e.tensor_copy(stats[:, s + 12 + h:s + 13 + h],
                                  ibln[:, h:h + 1].bitcast(F32))
                e.drain()
                for _ in range(2):
                    for h in range(2):
                        e.scalar_tensor_tensor(
                            out=stats[:, s + 6 + h:s + 7 + h],
                            in0=stats[:, s + 12 + h:s + 13 + h],
                            scalar=stats[:, s + 10 + h:s + 11 + h],
                            in1=stats[:, s + 12 + h:s + 13 + h],
                            op0=ALU.mult, op1=ALU.mult)
                    e.drain()
                    for h in range(2):
                        e.tensor_scalar(out=stats[:, s + 6 + h:s + 7 + h],
                                        in0=stats[:, s + 6 + h:s + 7 + h],
                                        scalar1=-0.5, scalar2=None, op0=ALU.mult)
                    e.drain()
                    for h in range(2):
                        e.tensor_scalar(out=stats[:, s + 6 + h:s + 7 + h],
                                        in0=stats[:, s + 6 + h:s + 7 + h],
                                        scalar1=1.5, scalar2=None, op0=ALU.add)
                    e.drain()
                    for h in range(2):
                        e.tensor_scalar(out=stats[:, s + 12 + h:s + 13 + h],
                                        in0=stats[:, s + 12 + h:s + 13 + h],
                                        scalar1=stats[:, s + 6 + h:s + 7 + h],
                                        scalar2=None, op0=ALU.mult)
                    e.drain()
                last = None
                for h in range(2):
                    last = e.tensor_scalar(out=stats[:, s + 14 + h:s + 15 + h],
                                           in0=stats[:, s + 4 + h:s + 5 + h],
                                           scalar1=stats[:, s + 12 + h:s + 13 + h],
                                           scalar2=None, op0=ALU.mult)
                return last
            tok_rstd[j] = E("vector", rstdchain, waits=[t_var], inc="cv")

            t_xn = [None, None]
            for h in range(2):
                t_xn[h] = E("scalar", lambda e, b=cur_x, h=h, s=sc: act(
                    e, xn_tm[:, h, :], b[:, h, :], ACT.Identity,
                    scale=stats[:, s + 12 + h:s + 13 + h],
                    bias=stats[:, s + 14 + h:s + 15 + h]),
                    waits=[tok_rstd[j],
                           tok_xnT[j - 1] if j >= 1 else None] if h == 0 else (),
                    inc="ca")

            return t_xn

        def emit_LN_Q(j):
            par = j % 2
            sc = 32 * par

            t_amax = None
            for h in range(2):
                t_amax = E("vector", lambda e, s=sc, h=h: e.tensor_reduce(
                    out=stats[:, s + 16 + h:s + 17 + h], in_=xn_tm[:, h, :],
                    axis=AX, op=ALU.max, apply_absolute_value=True),
                    waits=[tok_xn[j][1]] if h == 0 else (),
                    inc="cv" if h == 1 else None)

            def qshchain(e, s=sc):
                for h in range(2):
                    e.tensor_scalar(out=stats[:, s + 18 + h:s + 19 + h],
                                    in0=stats[:, s + 16 + h:s + 17 + h],
                                    scalar1=1e-12, scalar2=None, op0=ALU.max)
                e.drain()
                for h in range(2):
                    e.reciprocal(stats[:, s + 20 + h:s + 21 + h],
                                 stats[:, s + 18 + h:s + 19 + h])
                for h in range(2):
                    e.tensor_scalar(out=stats[:, s + 26 + h:s + 27 + h],
                                    in0=stats[:, s + 18 + h:s + 19 + h],
                                    scalar1=sm[:, SM_WS:SM_WS + 1],
                                    scalar2=None, op0=ALU.mult)
                e.drain()
                for h in range(2):
                    e.tensor_scalar(out=stats[:, s + 22 + h:s + 23 + h],
                                    in0=stats[:, s + 20 + h:s + 21 + h],
                                    scalar1=224.0, scalar2=None, op0=ALU.mult)
                e.drain()
                last = None
                for h in range(2):
                    last = e.tensor_scalar(out=stats[:, s + 24 + h:s + 25 + h],
                                           in0=stats[:, s + 26 + h:s + 27 + h],
                                           scalar1=2.0 / 448.0, scalar2=None,
                                           op0=ALU.mult)
                return last
            t_qsh = E("vector", qshchain, waits=[t_amax], inc="cv")

            for h in range(2):
                E("scalar", lambda e, h=h, s=sc: act(
                    e, q8_tm[:, h, :], xn_tm[:, h, :], ACT.Copy,
                    scale=stats[:, s + 22 + h:s + 23 + h]),
                    waits=[t_qsh, tok_qT[j - 1] if j >= 1 else None,
                           tok_mix[j - 1] if j >= 1 else None]
                    if h == 0 else ())
            E("scalar", lambda e: e.drain())
            t_qs8 = [None, None]
            for h in range(2):
                t_qs8[h] = E("scalar", lambda e, h=h, s=sc: act(
                    e, qs_tm[:, h, :], q8_tm[:, h, :], ACT.Copy,
                    scale=stats[:, s + 24 + h:s + 25 + h]), inc="ca")
            tok_qs[j] = t_qs8[1]
            return t_qs8

        def emit_transposes(j, t_qs8):
            cur_f = xn_f[j % 2]
            ca_xn = ("ca", sch.counts["ca"])
            ev_xn = [None] * 4
            t_xnT_ct = None
            for r in range(4):
                h = r // 2
                clo = (r % 2) * 4
                bank = pXT if r % 2 == 0 else pW[0]
                bnk = bk_XT if r % 2 == 0 else bk_W[0]
                tw = [ca_xn, bnk.producer_wait(), t_idf if j == 0 else None]
                tt = None
                for i in range(4):
                    tt = E("tensor", lambda e, h=h, c=clo + i, b=bank, i=i: e.transpose(
                        b[:, 128 * i:128 * (i + 1)],
                        xn_tm[:, h, 128 * c:128 * (c + 1)], identf[:, :]),
                        waits=tw if i == 0 else (),
                        inc="ct" if i == 3 else None)
                t_xnT_ct = tt
                ev_xn[r] = E("scalar", lambda e, b=bank, h=h, clo=clo, f=cur_f:
                             act(e, f[:, clo:clo + 4, 128 * h:128 * (h + 1)],
                                 b[:, :].rearrange("p (a b) -> p a b", b=128),
                                 ACT.Copy),
                             waits=[tt], inc="ca")
                bnk.consumed(ev_xn[r])
            tok_xnT[j] = t_xnT_ct
            tok_evxn[j] = ev_xn[3]
            ev_q = [None, None]
            t_qT_ct = None
            for h in range(2):
                tw = [t_qs8[h], bk_QT.producer_wait(), t_idb if j == 0 else None]
                tt = None
                for c in range(C):
                    tt = E("tensor", lambda e, h=h, c=c: e.transpose(
                        pQT[:, 128 * c:128 * (c + 1)],
                        qs_tm[:, h, 128 * c:128 * (c + 1)], identb[:, :]),
                        waits=tw if c == 0 else (),
                        inc="ct" if c == C - 1 else None)
                t_qT_ct = tt
                ev_q[h] = E("scalar", lambda e, h=h, qq=q_f[j % 2]: act(
                    e, qq[:, :, 128 * h:128 * (h + 1)],
                    pQT[:, :].rearrange("p (a b) -> p a b", b=128), ACT.Copy),
                    waits=[tt, tok_ffn[j - 2] if (j >= 2 and h == 0) else None],
                    inc="ca")
                bk_QT.consumed(ev_q[h])
            tok_evq[j] = ev_q[1]
            tok_qT[j] = t_qT_ct

        tok_xn[0] = emit_LN(0)
        t_qs8_0 = emit_LN_Q(0)
        emit_transposes(0, t_qs8_0)

        for j in range(NT):
            par = j % 2
            hb = 512 * par
            hbo = 512 * (1 - par)
            cur_x = x_tm[j % 2]
            cur_f = xn_f[j % 2]
            prv_f = xn_f[(j + 1) % 2]
            if j + 1 < NT:
                emit_LN_A(j + 1)

            cur_c = xcar[j % 2]
            if j > 0:
                E("vector", lambda e, cc=cur_c, p=prv_f: (
                    e.tensor_copy(cc[:, :, 0:1], p[:, :, TT - 1:TT]),
                    e.drain())[-1], waits=[tok_evxn[j]])

            def dxnchain(e, f=cur_f, cc=cur_c):
                e.tensor_tensor(out=dxn[:, :, 1:TT], in0=f[:, :, 1:TT],
                                in1=f[:, :, 0:TT - 1], op=ALU.subtract)
                for c in range(C):
                    e.tensor_scalar(out=dxn[:, c, 0:1], in0=f[:, c, 0:1],
                                    scalar1=cc[:, c, :], scalar2=None,
                                    op0=ALU.subtract)
                return e.drain()
            t_dxn = E("vector", dxnchain, waits=[tok_evxn[j]], inc="cv")
            mix_toks = {}
            mix_buf = {"g": mixA, "r": mixB, "k": mixA, "v": mixB}
            mix_m = {"g": SM_MG1, "r": SM_MR1, "k": SM_MK1, "v": SM_MV1}
            proj_done = {}
            kevac_runs = [None] * 4
            sg_runs = [None] * 4
            for pi, pk in enumerate("grkv"):
                buf = mix_buf[pk]
                wbuf = proj_done.get({"k": "g", "v": "r"}.get(pk))
                if wbuf is None:
                    wbuf = prev_proj_done.get({"g": "k", "r": "v"}.get(pk))
                tmx = None
                for c in range(C):
                    tmx = E("vector", lambda e, c=c, buf=buf, f=cur_f, mcol=mix_m[pk]:
                            e.scalar_tensor_tensor(
                                out=buf[:, c, :], in0=dxn[:, c, :],
                                scalar=sm[:, mcol + c:mcol + c + 1],
                                in1=f[:, c, :], op0=ALU.mult, op1=ALU.add),
                            waits=([wbuf] + ([t_mneg] if j == 0 else []))
                            if c == 0 else (),
                            inc="cv" if c == C - 1 else None)
                mix_toks[pk] = tmx

                for r in range(4):
                    gri = pi * 4 + r
                    bank = pP[gri % 2]
                    bnk = bk_P[gri % 2]
                    tw = [mix_toks[pk], bnk.producer_wait(),
                          t_w[pk] if j == 0 else None]
                    tt = None
                    for oo in range(2):
                        o = 2 * r + oo
                        for c in range(C):
                            tt = E("tensor", lambda e, c=c, o=o, oo=oo, buf=buf,
                                   bank=bank, pk=pk: e.matmul(
                                       bank[:, 256 * oo:256 * (oo + 1)],
                                       W[pk][:, c, 128 * o:128 * (o + 1)],
                                       buf[:, c, :], start=(c == 0), stop=(c == C - 1)),
                                   waits=tw if (oo == 0 and c == 0) else (),
                                   inc="ct" if (oo == 1 and c == C - 1) else None)
                    proj_done[pk] = tt
                    if pk == "g":
                        tev = None
                        for oo in range(2):
                            c = 2 * r + oo
                            tev = E("scalar", lambda e, bank=bank, oo=oo, c=c:
                                    act(e, sgb[:, c, :],
                                        bank[:, 256 * oo:256 * (oo + 1)],
                                        ACT.Sigmoid,
                                        bias=sm[:, SM_BG + c:SM_BG + c + 1]),
                                    waits=[tt, tok_gated[j - 1] if j >= 1 and r == 0
                                           and oo == 0 else None],
                                    inc="ca" if oo == 1 else None)
                        bnk.consumed(tev)
                        sg_runs[r] = tev
                    elif pk == "r":
                        tev = None
                        for oo in range(2):
                            c = 2 * r + oo
                            tev = E("scalar", lambda e, bank=bank, oo=oo, c=c:
                                    act(e, rbuf[:, c, :],
                                        bank[:, 256 * oo:256 * (oo + 1)],
                                        ACT.Identity,
                                        bias=sm[:, SM_BR + c:SM_BR + c + 1]),
                                    waits=[tt] if oo == 0 else (),
                                    inc="ca" if oo == 1 else None)
                        bnk.consumed(tev)
                    elif pk == "k":
                        tev = None
                        for oo in range(2):
                            c = 2 * r + oo
                            tev = E("scalar", lambda e, bank=bank, oo=oo, c=c:
                                    act(e, kbuf[:, c, :],
                                        bank[:, 256 * oo:256 * (oo + 1)],
                                        ACT.Identity,
                                        bias=sm[:, SM_BK + c:SM_BK + c + 1]),
                                    waits=[tt] if oo == 0 else (),
                                    inc="ca" if oo == 1 else None)
                        bnk.consumed(tev)
                        kevac_runs[r] = tev
                    else:
                        tev = None
                        for oo in range(2):
                            c = 2 * r + oo
                            tev = E("vector", lambda e, bank=bank, oo=oo, c=c:
                                    e.scalar_tensor_tensor(
                                        out=kbuf[:, c, :],
                                        in0=bank[:, 256 * oo:256 * (oo + 1)],
                                        scalar=sm[:, SM_BV + c:SM_BV + c + 1],
                                        in1=kbuf[:, c, :], op0=ALU.add, op1=ALU.mult),
                                    waits=[tt, kevac_runs[r]] if oo == 0 else (),
                                    inc="cv" if oo == 1 else None)
                        bnk.consumed(tev)
                if pk == "r" and j + 1 < NT:
                    tok_xn[j + 1] = emit_LN(j + 1)
            prev_proj_done = dict(proj_done)
            tok_mix[j] = mix_toks["v"]
            if j + 1 < NT:
                t_qs8_n = emit_LN_Q(j + 1)

            for c in range(C):
                E("vector", lambda e, c=c: e.tensor_tensor_scan(
                    out=accb[:, c, :],
                    data0=decv[:, c:c + 1].broadcast_to([128, TT]),
                    data1=kbuf[:, c, :],
                    initial=carry[:, c, :], op0=ALU.mult, op1=ALU.add),
                    waits=[t_dec, t_blo if j == 0 else None,
                           tok_sqmm[j - 1] if j >= 1 else None]
                    if c == 0 else ())

            def carrycopy(e):
                e.drain()
                return e.tensor_copy(carry[:, :, 0:1], accb[:, :, TT - 1:TT])
            t_carry = E("vector", carrycopy, inc="cv")

            t_g2m = [None] * 4
            slot_banks = [(pW[0], bk_W[0]), (pW[1], bk_W[1]), (pHB, bk_HB)]
            for s_i in range(4):
                h, kk = s_i // 2, s_i % 2
                bank, bnk = slot_banks[s_i % 3]
                tw = [tok_evq[j], bnk.producer_wait(),
                      t_w8cast if j == 0 else None, t_blo if j == 0 else None]
                tt = E("tensor", lambda e, bank=bank, kk=kk: e.matmul(
                    bank[:, :], ones2[:, :],
                    bias_ops[:, 1024 + 512 * kk:1024 + 512 * (kk + 1)],
                    start=True, stop=False), waits=tw)
                for c in range(C):
                    tt = E("tensor", lambda e, c=c, h=h, kk=kk, bank=bank,
                           qq=q_f[j % 2]: e.matmul(
                        bank[:, :], qq[:, c, 128 * h:128 * (h + 1)],
                        W8[:, c, 512 * kk:512 * (kk + 1)],
                        start=False, stop=(c == C - 1)),
                        inc="ct" if c == C - 1 else None)
                tok_ffn[j] = tt
                tmp_war = (t_g2m[s_i - 2] if s_i >= 2 else
                           (slot_resid_all[j - 1][s_i + 2] if j >= 1 else None))
                t_sig = E("scalar", lambda e, bank=bank, si=s_i: act(
                    e, tmp[:, si % 2, :], bank[:, :], ACT.Sigmoid),
                    waits=[tt, tmp_war], inc="ca")
                t_g2m[s_i] = E("vector", lambda e, bank=bank, si=s_i, h=h, kk=kk:
                               e.scalar_tensor_tensor(
                                   out=outb[:, h, 512 * kk:512 * (kk + 1)],
                                   in0=bank[:, :], scalar=1.0,
                                   in1=tmp[:, si % 2, :],
                                   op0=ALU.mult, op1=ALU.mult),
                               waits=[t_sig, tok_store[j - 1] if (j >= 1 and s_i == 0)
                                      else None],
                               inc="cv")
                bnk.consumed(t_g2m[s_i])

            if j + 1 < NT:
                emit_transposes(j + 1, t_qs8_n)

            for c in range(C):
                E("vector", lambda e, c=c: e.scalar_tensor_tensor(
                    out=kbuf[:, c, :], in0=kbuf[:, c, :],
                    scalar=sm[:, SM_U + c:SM_U + c + 1], in1=accb[:, c, :],
                    op0=ALU.mult, op1=ALU.add),
                    waits=[t_carry] if c == 0 else ())
            t_rw0 = E("vector", lambda e: (
                e.drain(),
                e.tensor_tensor(out=rbuf[:, 0:4, :], in0=rbuf[:, 0:4, :],
                                in1=kbuf[:, 0:4, :], op=ALU.mult))[-1], inc="cv")
            t_rw = E("vector", lambda e: e.tensor_tensor(
                out=rbuf[:, 4:8, :], in0=rbuf[:, 4:8, :],
                in1=kbuf[:, 4:8, :], op=ALU.mult), inc="cv")

            sqbf = (acc_flat.bitcast(BF16)[:, 0:2048]
                    .rearrange("p (c t) -> p c t", t=TT))
            t_sq0 = E("scalar", lambda e, s=sqbf: act(
                e, s[:, 0:4, :], rbuf[:, 0:4, :], ACT.Square),
                waits=[t_rw0], inc="ca")
            t_sqrw = E("scalar", lambda e, s=sqbf: act(
                e, s[:, 4:8, :], rbuf[:, 4:8, :], ACT.Square),
                waits=[t_rw], inc="ca")
            sq_fm = sqbf

            w_hs = [t_rw0, bk_HS.producer_wait(), t_bo if j == 0 else None]
            tt = None
            for c in range(C):
                tt = E("tensor", lambda e, c=c: e.matmul(
                    pHS[:, 0:256], bones[:, 16 * c:16 * (c + 1)],
                    rbuf[:, c, :], start=(c == 0), stop=(c == C - 1)),
                    waits=w_hs if c == 0 else
                    ([t_rw] if c == 4 else ()))
            for c in range(C):
                tt = E("tensor", lambda e, c=c: e.matmul(
                    pHS[:, 256:512], bones[:, 16 * c:16 * (c + 1)],
                    sq_fm[:, c, :], start=(c == 0), stop=(c == C - 1)),
                    waits=[t_sq0] if c == 0 else
                    ([t_sqrw] if c == 4 else ()),
                    inc="ct" if c == C - 1 else None)
            t_hs = tt
            tok_sqmm[j] = t_hs

            t_hvar = E("vector", lambda e, hb=hb, hbo=hbo: (
                e.tensor_scalar(out=hbuf[:, hb:hb + 512], in0=pHS[:, 0:512],
                                scalar1=1.0 / HD, scalar2=None, op0=ALU.mult),
                e.drain(),
                e.tensor_tensor(out=hbuf[:, hbo:hbo + 256],
                                in0=hbuf[:, hb:hb + 256], in1=hbuf[:, hb:hb + 256],
                                op=ALU.mult),
                e.drain(),
                e.scalar_tensor_tensor(out=hbuf[:, hb + 256:hb + 512],
                                       in0=hbuf[:, hb + 256:hb + 512],
                                       scalar=1e-5,
                                       in1=hbuf[:, hbo:hbo + 256],
                                       op0=ALU.add, op1=ALU.subtract))[-1],
                waits=[t_hs], inc="cv")
            bk_HS.consumed(t_hvar)

            def hrsdchain(e, hb=hb, hbo=hbo):
                vx = hbuf[0:16, hb + 256:hb + 512]
                y = hbuf[0:16, hbo + 256:hbo + 512]
                p = hbuf[0:16, hbo:hbo + 256]
                e.tensor_scalar(out=ibh[:, :], in0=vx.bitcast(mybir.dt.int32),
                                scalar1=1, scalar2=-1,
                                op0=ALU.logical_shift_right,
                                op1=ALU.bitwise_xor)
                e.drain()
                e.tensor_scalar(out=ibh[:, :], in0=ibh[:, :], scalar1=MAGIC + 1,
                                scalar2=None, op0=ALU.add)
                e.drain()
                e.tensor_copy(y, ibh[:, :].bitcast(F32))
                e.drain()
                for _ in range(2):
                    e.tensor_tensor(out=p, in0=y, in1=y, op=ALU.mult)
                    e.drain()
                    e.tensor_tensor(out=p, in0=p, in1=vx, op=ALU.mult)
                    e.drain()
                    e.tensor_scalar(out=p, in0=p, scalar1=-0.5, scalar2=1.5,
                                    op0=ALU.mult, op1=ALU.add)
                    e.drain()
                    e.tensor_tensor(out=y, in0=y, in1=p, op=ALU.mult)
                    e.drain()
                e.drain()
                bst = ibh[:, :].bitcast(BF16)
                e.tensor_copy(bst[:, 0:256], hbuf[0:16, hb:hb + 256])
                return e.tensor_copy(bst[:, 256:512], y)
            t_hrsd = E("vector", hrsdchain, waits=[t_hvar], inc="cv")

            t_lnc = {}
            for c in range(C):
                bank = pHB if c % 2 == 0 else pW[1]
                bnk = bk_HB if c % 2 == 0 else bk_W[1]
                tw = [t_hrsd, bnk.producer_wait(), t_sel if j == 0 else None]
                tb = E("tensor", lambda e, c=c, bank=bank: e.matmul(
                    bank[:, 0:512], sel[:, 128 * c:128 * (c + 1)],
                    ibh[:, :].bitcast(BF16), start=True, stop=True),
                    waits=tw, inc="ct")
                w_g = [tb, sg_runs[c // 2]]
                tg = E("vector", lambda e, c=c, bank=bank: (
                    e.tensor_tensor(out=gat[:, c, :], in0=rbuf[:, c, :],
                                    in1=bank[:, 0:256], op=ALU.subtract),
                    e.drain(),
                    e.tensor_tensor(out=gat[:, c, :], in0=gat[:, c, :],
                                    in1=bank[:, 256:512], op=ALU.mult),
                    e.drain(),
                    e.tensor_tensor(out=gat[:, c, :], in0=sgb[:, c, :],
                                    in1=gat[:, c, :], op=ALU.mult))[-1],
                    waits=w_g, inc="cv")
                t_lnc[c] = tg
                bnk.consumed(tg)
            tok_gated[j] = t_lnc[C - 1]

            for s_i in range(4):
                h, kk = s_i // 2, s_i % 2
                bank, bnk = slot_banks[s_i % 3]
                tw2 = [tok_gated[j], bnk.producer_wait(),
                       t_w["o"] if j == 0 else None]
                tt = E("tensor", lambda e, bank=bank, kk=kk: e.matmul(
                    bank[:, :], ones2[:, :],
                    bias_ops[:, 512 * kk:512 * (kk + 1)],
                    start=True, stop=False), waits=tw2)
                for c in range(C):
                    tt = E("tensor", lambda e, c=c, h=h, kk=kk, bank=bank: e.matmul(
                        bank[:, :], gat[:, c, 128 * h:128 * (h + 1)],
                        W["o"][:, c, 512 * kk:512 * (kk + 1)],
                        start=False, stop=(c == C - 1)),
                        inc="ct" if c == C - 1 else None)
                if s_i == 3:
                    tok_wo[j] = tt
                t_res = E("vector", lambda e, bank=bank, h=h, kk=kk, si=s_i,
                          b=cur_x: (
                    e.tensor_tensor(out=tmp[:, si % 2, :], in0=bank[:, :],
                                    in1=outb[:, h, 512 * kk:512 * (kk + 1)],
                                    op=ALU.mult),
                    e.drain(),
                    e.tensor_tensor(out=outb[:, h, 512 * kk:512 * (kk + 1)],
                                    in0=tmp[:, si % 2, :],
                                    in1=b[:, h, 512 * kk:512 * (kk + 1)],
                                    op=ALU.add))[-1],
                    waits=[tt], inc="cv")
                bnk.consumed(t_res)
                slot_resid_all[j][s_i] = t_res
                if s_i == 3:
                    resid_done[j] = t_res

            dstv = out_h[j * TT:(j + 1) * TT, :].rearrange("(h p) d -> p h d", p=128)
            tok_store[j] = E("gpsimd", lambda e, d=dstv: e.dma_start(
                out=d, in_=outb[:, :, :]), waits=[resid_done[j]],
                inc="co", inc_n=16)

            if j + 2 < NT:
                emit_xload(j + 2)

        E("gpsimd", lambda e: None, waits=[("co", sch.counts["co"])])

        with nc.Block() as block:
            for engn, deco in [("sync", block.sync), ("gpsimd", block.gpsimd),
                               ("tensor", block.tensor), ("vector", block.vector),
                               ("scalar", block.scalar)]:
                def body(eng, engn=engn):
                    for (en, fn, waits, inc_sem, inc_n) in sch.ops:
                        if en != engn:
                            continue
                        for (semname, val) in waits:
                            eng.wait_ge(sch.sems[semname], val)
                        inst = fn(eng)
                        if inc_n:
                            if engn in ("vector", "scalar"):
                                inst = eng.drain()
                            elif inst is None:
                                inst = eng.nop()
                            inst.then_inc(inc_sem, inc_n)
                deco(body)
    return nc


def _host_prep(inputs):
    f32 = np.float32
    x = np.ascontiguousarray(inputs["L0_attn_x"])
    per = {}
    per["wgT"] = np.ascontiguousarray(inputs["L0_attn_Wg_w"].T)
    per["wrT"] = np.ascontiguousarray(inputs["L0_attn_Wr_w"].T)
    per["wkT"] = np.ascontiguousarray(inputs["L0_attn_Wk_w"].T)
    per["wvT"] = np.ascontiguousarray(inputs["L0_attn_Wv_w"].T)
    per["woT"] = np.ascontiguousarray(inputs["L0_attn_Wo_w"].T)
    per["w8T"] = np.ascontiguousarray(
        inputs["L0_ffn_weight"].T.astype(np.float32))

    def cols(v):
        return np.ascontiguousarray(np.asarray(v, f32).reshape(C, 128).T)

    sm = np.zeros((128, SM_COLS), f32)
    sm[:, SM_MG:SM_MG + 8] = cols(inputs["L0_attn_mg"])
    sm[:, SM_MR:SM_MR + 8] = cols(inputs["L0_attn_mr"])
    sm[:, SM_MK:SM_MK + 8] = cols(inputs["L0_attn_mk"])
    sm[:, SM_MV:SM_MV + 8] = cols(inputs["L0_attn_mv"])
    sm[:, SM_BG:SM_BG + 8] = cols(inputs["L0_attn_Wg_b"])
    sm[:, SM_BR:SM_BR + 8] = cols(inputs["L0_attn_Wr_b"])
    sm[:, SM_BK:SM_BK + 8] = cols(inputs["L0_attn_Wk_b"])
    sm[:, SM_BV:SM_BV + 8] = cols(inputs["L0_attn_Wv_b"])
    sm[:, SM_W:SM_W + 8] = cols(np.asarray(inputs["L0_attn_w"], f32).reshape(-1))
    sm[:, SM_U:SM_U + 8] = cols(np.asarray(inputs["L0_attn_u"], f32).reshape(-1))
    sm[:, SM_WS] = np.float32(inputs["L0_ffn_weight_scale"])
    sm[:, SM_ONE] = 1.0
    per["smalls"] = sm

    pack2 = np.zeros((1, 2048), f32)
    pack2[0, 0:1024] = np.asarray(inputs["L0_attn_Wo_b"], f32)
    pack2[0, 1024:2048] = np.asarray(inputs["L0_ffn_bias"], f32)
    per["pack2"] = pack2

    per["identf"] = np.eye(128, dtype=f32)
    per["identb"] = np.eye(128).astype(ml_dtypes.bfloat16)
    bones = np.zeros((128, 128), f32)
    for c in range(C):
        for p in range(128):
            bones[p, 16 * c + (2 * c + p // 64)] = 1.0
    per["bones"] = bones.astype(ml_dtypes.bfloat16)
    sel = np.zeros((16, 1024), f32)
    for c in range(C):
        for m in range(128):
            sel[2 * c + m // 64, 128 * c + m] = 1.0
    per["sel"] = sel.astype(ml_dtypes.bfloat16)
    return x, per


def kernel(**inputs):
    x, per = _host_prep(inputs)
    B, N, _ = x.shape
    NT = N // TT
    nc = build(NT=NT)
    in_maps = []
    for b in range(B):
        m = {"x": np.ascontiguousarray(x[b]),
             "xlast": np.ascontiguousarray(x[b, N - 1:N, :])}
        m.update(per)
        in_maps.append(m)
    res = run_bass_kernel_spmd(nc, in_maps, core_ids=list(range(B)))
    out = np.stack([res.results[b]["out"] for b in range(B)], axis=0)
    return out.astype(np.float32)


# version 28 (final)
# speedup vs baseline: 1.0044x; 1.0044x over previous
import contextlib

import numpy as np
import ml_dtypes

import concourse.bass as bass
import concourse.mybir as mybir
from concourse.bass_utils import run_bass_kernel_spmd

F32 = mybir.dt.float32
F32R = mybir.dt.float32r
BF16 = mybir.dt.bfloat16
F8 = mybir.dt.float8e4
ACT = mybir.ActivationFunctionType
ALU = mybir.AluOpType
AX = mybir.AxisListType.X

D = 1024
H = 16
HD = 64
C = 8
TT = 256
NCORES = 8

SM_MG, SM_MR, SM_MK, SM_MV = 0, 8, 16, 24
SM_BG, SM_BR, SM_BK, SM_BV = 32, 40, 48, 56
SM_W, SM_U = 64, 72
SM_WS = 80
SM_ONE = 81
SM_MG1, SM_MR1, SM_MK1, SM_MV1 = 88, 96, 104, 112
SM_COLS = 120

NOPS = 220
DEBUG_NOLUT = False
DEBUG_NOSCAN = False
STOP_LEVEL = 9


class Sched:

    def __init__(self, nc):
        self.nc = nc
        self.ops = []
        self.sems = {}
        self.counts = {}

    def sem(self, name):
        if name not in self.sems:
            self.sems[name] = self.nc.alloc_semaphore(name=name)
            self.counts[name] = 0
        return self.sems[name]

    def emit(self, eng, fn, waits=(), inc=None, inc_n=1):
        best = {}
        for x in waits:
            if x is None or x[1] <= 0:
                continue
            if x[0] not in best or x[1] > best[x[0]]:
                best[x[0]] = x[1]
        w = list(best.items())
        tok = None
        if inc is not None:
            self.sem(inc)
            if inc_n == 16:
                prev = self.counts[inc]
                if prev > 0 and not best.get(inc, 0) >= prev:
                    w.append((inc, prev))
            self.counts[inc] += inc_n
            tok = (inc, self.counts[inc])
        self.ops.append((eng, fn, w, self.sems.get(inc), inc_n if inc else 0))
        return tok


class Bank:

    def __init__(self, name):
        self.name = name
        self.free_tok = None

    def producer_wait(self):
        return self.free_tok

    def consumed(self, tok):
        self.free_tok = tok


def build(NT=32, taps=(), tap_tile=0):
    N = NT * TT
    nc = bass.Bass("TRN2", target_bir_lowering=False, debug=False)

    x_h = nc.dram_tensor("x", [N, D], F32, kind="ExternalInput").ap()
    wT_h = {k: nc.dram_tensor(f"w{k}T", [D, D], F32, kind="ExternalInput").ap()
            for k in "grkvo"}
    w8T_h = nc.dram_tensor("w8T", [D, D], F32, kind="ExternalInput").ap()
    sm_h = nc.dram_tensor("smalls", [128, SM_COLS], F32, kind="ExternalInput").ap()
    pack2_h = nc.dram_tensor("pack2", [1, 2048], F32, kind="ExternalInput").ap()
    identf_h = nc.dram_tensor("identf", [128, 128], F32, kind="ExternalInput").ap()
    identb_h = nc.dram_tensor("identb", [128, 128], BF16, kind="ExternalInput").ap()
    bones_h = nc.dram_tensor("bones", [128, 128], BF16, kind="ExternalInput").ap()
    sel_h = nc.dram_tensor("sel", [16, 1024], BF16, kind="ExternalInput").ap()
    xlast_h = nc.dram_tensor("xlast", [1, D], F32, kind="ExternalInput").ap()
    out_h = nc.dram_tensor("out", [N, D], F32, kind="ExternalOutput").ap()
    tap_h = {}
    tap_specs = {
        "xn_tm": ([128, 2, D], F32), "qs_tm": ([128, 2, D], BF16),
        "xn_f": ([128, C, TT + 1], BF16), "q_f": ([128, C, TT], BF16),
        "mixA": ([128, C, TT], BF16), "mixB": ([128, C, TT], BF16),
        "kv": ([128, C, TT], F32), "acc": ([128, C, TT], F32),
        "rw": ([128, C, TT], F32), "gat": ([128, C, TT], BF16),
        "sgb": ([128, C, TT], BF16), "hbuf": ([16, 1024], F32),
        "stats": ([128, 64], F32), "bias_ops": ([2, 2048], BF16),
        "rbuf_r": ([128, C, TT], F32), "kbuf_k": ([128, C, TT], F32),
        "decr": ([128, C, TT], F32), "w8sb": ([128, C, D], BF16),
    }
    for t in taps:
        tap_h[t] = nc.dram_tensor("tap_" + t, tap_specs[t][0], tap_specs[t][1],
                                  kind="ExternalOutput").ap()

    sch = Sched(nc)
    for name in ["go", "cx", "cw", "cw1", "cw2", "ct", "cv", "ca", "co", "cg", "g2", "g3", "g4"]:
        sch.sem(name)

    es = contextlib.ExitStack()
    with es:
        sb = lambda name, shape, dt: es.enter_context(nc.sbuf_tensor(name, shape, dt))
        ps = lambda name, shape, dt: es.enter_context(nc.psum_tensor(name, shape, dt))

        W = {k: sb(f"W{k}", [128, C, D], BF16) for k in "grkvo"}
        W8 = sb("W8", [128, C, D], BF16)
        sm = sb("sm", [128, SM_COLS], F32)
        x_tm = [sb(f"x{i}", [128, 2, D], F32) for i in range(2)]
        xn_tm = sb("xn", [128, 2, D], F32)
        qs_tm = sb("qs", [128, 2, D], BF16)
        xn_f = [sb(f"xnf{i}", [128, C, TT], BF16) for i in range(2)]
        xcar = [sb(f"xcar{i}", [128, C, 1], F32) for i in range(2)]
        q_f = [sb(f"qf{i}", [128, C, TT], BF16) for i in range(2)]
        dxn = sb("dxn", [128, C, TT], BF16)
        mixA = sb("mixA", [128, C, TT], BF16)
        mixB = sb("mixB", [128, C, TT], BF16)
        sgb = sb("sgb", [128, C, TT], BF16)
        rbuf = sb("rbuf", [128, C, TT], BF16)
        kbuf = sb("kbuf", [128, C, TT], F32)
        accb = sb("accb", [128, C, TT], F32)
        carry = sb("carry", [128, C, 1], F32)
        gat = sb("gat", [128, C, TT], BF16)
        tmp = sb("tmp", [128, 2, 512], F32)
        outb = sb("outb", [128, 2, D], F32)
        stats = sb("stats", [128, 64], F32)
        hbuf = sb("hbuf", [16, 1024], F32)
        bias_ops = sb("bias_ops", [2, 2048], BF16)
        ones2 = sb("ones2", [2, 128], BF16)
        decv = sb("decv", [128, C], F32)
        ibln = sb("ibln", [128, 4], mybir.dt.int32)
        ibh = sb("ibh", [16, 256], mybir.dt.int32)
        identf = sb("identf_s", [128, 128], F32)
        identb = sb("identb_s", [128, 128], BF16)
        bones = sb("bones_s", [128, 128], BF16)
        sel = sb("sel_s", [16, 1024], BF16)
        lst = sb("lst", [1, 16], F32)
        q8_tm = (dxn[:, 0:4, :].rearrange("p c t -> p (c t)").bitcast(F8)
                 .rearrange("p (h d) -> p h d", d=D))
        rb_flat = rbuf[:, :, :].rearrange("p c t -> p (c t)")
        acc_flat = accb[:, :, :].rearrange("p c t -> p (c t)")
        p2row = acc_flat[0:1, 0:2048]
        xlast = outb[0:1, 0, :]
        xlast2 = outb[0:1, 1, :]

        pXT = ps("pXT", [128, 512], F32)
        pQT = ps("pQT", [128, 1024], BF16)
        pP = [ps("pP1", [128, 512], F32), ps("pP2", [128, 512], F32)]
        pHS = ps("pHS", [16, 512], F32)
        pHB = ps("pHB", [128, 512], F32)
        pW = [ps("pW1", [128, 512], F32), ps("pW2", [128, 512], F32)]

        bk_P = [Bank("pP1"), Bank("pP2")]
        bk_XT = Bank("pXT")
        bk_QT = Bank("pQT")
        bk_HS = Bank("pHS")
        bk_HB = Bank("pHB")
        bk_W = [Bank("pW1"), Bank("pW2")]

        E = sch.emit
        NF1 = np.float32(1.0)
        NF0 = np.float32(0.0)
        MAGIC = 0x5F3759DF

        def act(e, out, in_, func, scale=None, bias=None, accum_out=None):
            if func == ACT.Copy:
                func = ACT.Identity
            if DEBUG_NOLUT and func in (ACT.Sqrt, ACT.Sigmoid, ACT.Silu, ACT.Exp):
                func = ACT.Identity
            return e.activation(out=out, in_=in_, func=func,
                                scale=NF1 if scale is None else scale,
                                bias=NF0 if bias is None else bias,
                                accum_out=accum_out)

        def f32r(ap):
            return ap

        def clears(eng):
            for s in range(155, 220):
                eng.sem_clear(range(s, s + 1))
            return None
        E("sync", clears)
        E("sync", lambda e: e.sem_inc(sch.sems["go"], 1))
        go = ("go", 1)

        def dma(dst, src):
            return lambda e: e.dma_start(out=dst, in_=src)

        t_sm = E("sync", dma(sm[:, :], sm_h), inc="cx", inc_n=16)
        t_p2 = E("sync", dma(p2row, pack2_h), inc="cx", inc_n=16)
        t_idf = E("sync", dma(identf[:, :], identf_h), inc="cx", inc_n=16)
        t_idb = E("sync", dma(identb[:, :], identb_h), inc="cx", inc_n=16)
        t_bo = E("sync", dma(bones[:, :], bones_h), inc="cx", inc_n=16)
        t_sel = E("sync", dma(sel[:, :], sel_h), inc="cx", inc_n=16)
        t_xl = E("sync", dma(xlast, xlast_h), inc="cx", inc_n=16)
        t_consts = t_xl

        t_x = [None] * NT
        resid_done = [None] * NT

        def emit_xload(j):
            src = x_h[j * TT:(j + 1) * TT, :].rearrange("(h p) d -> p h d", p=128)
            t_x[j] = E("sync", dma(x_tm[j % 2][:, :, :], src),
                       waits=[resid_done[j - 2] if j >= 2 else None],
                       inc="cx", inc_n=16)

        emit_xload(0)
        emit_xload(1)

        def delay(eng):
            for _ in range(400):
                eng.nop()
            return None
        E("gpsimd", delay)
        E("gpsimd", lambda e: None, waits=[go])
        go2 = E("gpsimd", lambda e: e.nop(), inc="g2")

        def gwait(e):
            return None
        t_w = {}
        wch = {"g": "cw", "r": "cw1", "k": "cw2", "v": "cw", "o": "cw1"}
        for k in "grkvo":
            t_w[k] = E("gpsimd", dma(W[k][:, :, :],
                                     wT_h[k].rearrange("(c p) o -> p c o", p=128)),
                       inc=wch[k], inc_n=16)
        t_w8cast = E("gpsimd", dma(W8[:, :, :],
                                   w8T_h.rearrange("(c p) o -> p c o", p=128)),
                     inc="cw2", inc_n=16)

        E("vector", delay)
        E("vector", lambda e: None, waits=[go, go2])
        go3 = E("vector", lambda e: e.nop(), inc="g3")
        E("scalar", delay)
        E("scalar", lambda e: None, waits=[go, go3])
        go4 = E("scalar", lambda e: e.nop(), inc="g4")
        E("tensor", delay)
        E("tensor", lambda e: None, waits=[go, go4])


        def mneg(e):
            for base_src, base_dst in [(SM_MG, SM_MG1), (SM_MR, SM_MR1),
                                       (SM_MK, SM_MK1), (SM_MV, SM_MV1)]:
                for c in range(C):
                    e.tensor_scalar(out=sm[:, base_dst + c:base_dst + c + 1],
                                    in0=sm[:, base_src + c:base_src + c + 1],
                                    scalar1=1.0, scalar2=None, op0=ALU.subtract)
            return e.drain()
        t_mneg = E("vector", mneg, waits=[t_sm], inc="cv")

        t_msets = E("vector", lambda e: (
            e.memset(carry[:, :, :], 0.0),
            e.memset(ones2[:, :], 1.0))[-1], inc="cv")
        E("scalar", lambda e: act(e, lst[0:1, 15:16], sm[0:1, 0:1], ACT.Sigmoid),
          waits=[t_sm])

        t_z1 = E("scalar",
                 lambda e: act(e, decv[:, :], sm[:, SM_W:SM_W + 8], ACT.Tanh,
                               scale=np.float32(0.5)),
                 waits=[t_sm], inc="ca")

        def neg_e1(e):
            for c in range(C):
                e.tensor_scalar(out=stats[:, 56 + c:57 + c], in0=decv[:, c:c + 1],
                                scalar1=1.0, scalar2=None, op0=ALU.add)
            e.drain()
            for c in range(C):
                e.tensor_scalar(out=decv[:, c:c + 1], in0=decv[:, c:c + 1],
                                scalar1=1.0, scalar2=None, op0=ALU.subtract)
            e.drain()
            for c in range(C):
                e.reciprocal(decv[:, c:c + 1], decv[:, c:c + 1])
            e.drain()
            for c in range(C):
                e.tensor_scalar(out=decv[:, c:c + 1], in0=decv[:, c:c + 1],
                                scalar1=stats[:, 56 + c:57 + c], scalar2=None,
                                op0=ALU.mult)
            return e.drain()
        t_ne1 = E("vector", neg_e1, waits=[t_z1], inc="cv")
        t_z2 = E("scalar",
                 lambda e: act(e, decv[:, :], decv[:, :], ACT.Tanh,
                               scale=np.float32(0.5)),
                 waits=[t_ne1], inc="ca")

        def decay_fin(e):
            for c in range(C):
                e.tensor_scalar(out=stats[:, 56 + c:57 + c], in0=decv[:, c:c + 1],
                                scalar1=1.0, scalar2=None, op0=ALU.add)
            e.drain()
            for c in range(C):
                e.tensor_scalar(out=decv[:, c:c + 1], in0=decv[:, c:c + 1],
                                scalar1=1.0, scalar2=None, op0=ALU.subtract)
            e.drain()
            for c in range(C):
                e.reciprocal(decv[:, c:c + 1], decv[:, c:c + 1])
            e.drain()
            for c in range(C):
                e.tensor_scalar(out=decv[:, c:c + 1], in0=decv[:, c:c + 1],
                                scalar1=stats[:, 56 + c:57 + c], scalar2=None,
                                op0=ALU.mult)
            e.drain()
            for c in range(C):
                e.tensor_scalar(out=decv[:, c:c + 1], in0=decv[:, c:c + 1],
                                scalar1=-1.0, scalar2=None, op0=ALU.mult)
            return e.drain()
        t_dec = E("vector", decay_fin, waits=[t_z2], inc="cv")
        t_bhi = E("scalar",
                  lambda e: act(e, bias_ops[0:1, :], p2row, ACT.Copy),
                  waits=[t_p2], inc="ca")
        blo_scr = acc_flat[0:1, 0:1024].bitcast(BF16)
        t_blo_s = E("vector",
                    lambda e: e.tensor_tensor(out=blo_scr, in0=p2row,
                                              in1=bias_ops[0:1, :], op=ALU.subtract),
                    waits=[t_bhi], inc="cv")
        t_blo = E("sync", lambda e: e.dma_start(out=bias_ops[1:2, :], in_=blo_scr),
                  waits=[t_blo_s], inc="cx", inc_n=16)

        t_lsq = E("scalar",
                  lambda e: act(e, xn_tm[0:1, 0, :], xlast, ACT.Square,
                                accum_out=lst[:, 2:3]),
                  waits=[t_xl], inc="ca")
        t_lsum = E("vector",
                   lambda e: e.tensor_reduce(out=lst[:, 0:1], in_=xlast,
                                             axis=AX, op=ALU.add, negate=True),
                   waits=[t_xl], inc="cv")
        def lvarchain(e):
            e.tensor_scalar(out=lst[:, 4:5], in0=lst[:, 0:1], scalar1=1.0 / D,
                            scalar2=None, op0=ALU.mult)
            e.tensor_scalar(out=lst[:, 8:9], in0=lst[:, 2:3], scalar1=1.0 / D,
                            scalar2=None, op0=ALU.mult)
            e.drain()
            e.tensor_scalar(out=lst[:, 6:7], in0=lst[:, 4:5],
                            scalar1=lst[0:1, 4:5], scalar2=None, op0=ALU.mult)
            e.drain()
            return e.tensor_scalar(out=lst[:, 8:9], in0=lst[:, 8:9],
                                   scalar1=lst[0:1, 6:7], scalar2=None,
                                   op0=ALU.subtract)
        t_lvar = E("vector", lvarchain, waits=[t_lsq], inc="cv")
        def lrschain(e):
            e.tensor_scalar(out=lst[:, 10:11], in0=lst[:, 8:9], scalar1=1e-5,
                            scalar2=None, op0=ALU.add)
            e.drain()
            e.tensor_scalar(out=ibln[0:1, 2:3],
                            in0=lst[:, 10:11].bitcast(mybir.dt.int32),
                            scalar1=1, scalar2=None, op0=ALU.logical_shift_right)
            e.drain()
            e.tensor_scalar(out=ibln[0:1, 2:3], in0=ibln[0:1, 2:3],
                            scalar1=-1, scalar2=None, op0=ALU.bitwise_xor)
            e.drain()
            e.tensor_scalar(out=ibln[0:1, 2:3], in0=ibln[0:1, 2:3],
                            scalar1=MAGIC + 1, scalar2=None, op0=ALU.add)
            e.drain()
            e.tensor_copy(lst[:, 12:13], ibln[0:1, 2:3].bitcast(F32))
            e.drain()
            for _ in range(4):
                e.scalar_tensor_tensor(out=lst[:, 6:7], in0=lst[:, 12:13],
                                       scalar=lst[0:1, 10:11], in1=lst[:, 12:13],
                                       op0=ALU.mult, op1=ALU.mult)
                e.drain()
                e.tensor_scalar(out=lst[:, 6:7], in0=lst[:, 6:7], scalar1=-0.5,
                                scalar2=None, op0=ALU.mult)
                e.drain()
                e.tensor_scalar(out=lst[:, 6:7], in0=lst[:, 6:7], scalar1=1.5,
                                scalar2=None, op0=ALU.add)
                e.drain()
                e.tensor_scalar(out=lst[:, 12:13], in0=lst[:, 12:13],
                                scalar1=lst[0:1, 6:7], scalar2=None, op0=ALU.mult)
                e.drain()
            return e.tensor_scalar(out=lst[:, 14:15], in0=lst[:, 4:5],
                                   scalar1=lst[0:1, 12:13], scalar2=None,
                                   op0=ALU.mult)
        t_lrs = E("vector", lrschain, waits=[t_lvar, t_msets], inc="cv")
        t_lxn = E("scalar",
                  lambda e: act(e, xlast2, xlast, ACT.Identity,
                                scale=lst[0:1, 12:13], bias=lst[0:1, 14:15]),
                  waits=[t_lrs], inc="ca")
        t_lcol = None
        for c in range(C):
            t_lcol = E("tensor",
                       lambda e, c=c: e.matmul(pXT[:, c:c + 1],
                                               f32r(xlast2[0:1, 128 * c:128 * (c + 1)]),
                                               f32r(sm[0:1, SM_ONE:SM_ONE + 1]),
                                               start=True, stop=True),
                       waits=[t_lxn, t_sm] if c == 0 else (),
                       inc="ct" if c == C - 1 else None)
        t_xcol = E("vector",
                   lambda e: e.tensor_copy(xcar[0][:, :, 0:1], pXT[:, 0:8]),
                   waits=[t_lcol], inc="cv")
        bk_XT.consumed(t_xcol)

        tok_qs = [None] * (NT + 1)
        tok_sqmm = [None] * (NT + 1)
        tok_xnT = [None] * (NT + 1)
        tok_qT = [None] * (NT + 1)
        tok_gated = [None] * (NT + 1)
        tok_store = [None] * (NT + 1)
        tok_evq = [None] * (NT + 1)
        tok_evxn = [None] * (NT + 1)
        tok_ffn = [None] * (NT + 1)
        tok_rstd = [None] * (NT + 1)
        tok_mix = [None] * (NT + 1)
        slot_resid_all = [[None] * 4 for _ in range(NT + 1)]
        tok_wo = [None] * (NT + 1)
        prev_proj_done = {}

        tok_ssq = [None] * (NT + 1)
        tok_xn = [None] * (NT + 1)

        def emit_LN_A(j):
            if tok_ssq[j] is not None:
                return
            par = j % 2
            sc = 32 * par
            cur_x = x_tm[j % 2]
            w_sq = [t_x[j], tok_qT[j - 1] if j >= 1 else None]
            for h in range(2):
                E("scalar", lambda e, b=cur_x, h=h, s=sc: act(
                    e, qs_tm[:, h, :], b[:, h, :],
                    ACT.Square, accum_out=stats[:, s + 2 + h:s + 3 + h]),
                    waits=w_sq if h == 0 else ())
            tok_ssq[j] = E("scalar", lambda e: e.drain(), inc="ca")

        def emit_LN(j):
            par = j % 2
            sc = 32 * par
            cur_x = x_tm[j % 2]
            w_ln = [t_x[j], tok_qs[j - 2] if j >= 2 else None]
            emit_LN_A(j)
            t_nsum = None
            for h in range(2):
                t_nsum = E("vector", lambda e, b=cur_x, s=sc, h=h: e.tensor_reduce(
                    out=stats[:, s + h:s + 1 + h], in_=b[:, h, :], axis=AX,
                    op=ALU.add, negate=True), waits=w_ln if h == 0 else (),
                    inc="cv" if h == 1 else None)
            t_ssq = tok_ssq[j]

            def varchain(e, s=sc):
                e.drain()
                for h in range(2):
                    e.tensor_scalar(out=stats[:, s + 4 + h:s + 5 + h],
                                    in0=stats[:, s + 0 + h:s + 1 + h],
                                    scalar1=1.0 / D, scalar2=None, op0=ALU.mult)
                for h in range(2):
                    e.tensor_scalar(out=stats[:, s + 8 + h:s + 9 + h],
                                    in0=stats[:, s + 2 + h:s + 3 + h],
                                    scalar1=1.0 / D, scalar2=None, op0=ALU.mult)
                e.drain()
                for h in range(2):
                    e.tensor_scalar(out=stats[:, s + 6 + h:s + 7 + h],
                                    in0=stats[:, s + 4 + h:s + 5 + h],
                                    scalar1=stats[:, s + 4 + h:s + 5 + h],
                                    scalar2=None, op0=ALU.mult)
                e.drain()
                last = None
                for h in range(2):
                    last = e.tensor_scalar(out=stats[:, s + 8 + h:s + 9 + h],
                                           in0=stats[:, s + 8 + h:s + 9 + h],
                                           scalar1=stats[:, s + 6 + h:s + 7 + h],
                                           scalar2=None, op0=ALU.subtract)
                return last
            t_var = E("vector", varchain, waits=[t_ssq, t_nsum], inc="cv")

            def rstdchain(e, s=sc):
                for h in range(2):
                    e.tensor_scalar(out=stats[:, s + 10 + h:s + 11 + h],
                                    in0=stats[:, s + 8 + h:s + 9 + h],
                                    scalar1=1e-5, scalar2=None, op0=ALU.add)
                e.drain()
                for h in range(2):
                    e.tensor_scalar(out=ibln[:, h:h + 1],
                                    in0=stats[:, s + 10 + h:s + 11 + h].bitcast(
                                        mybir.dt.int32),
                                    scalar1=1, scalar2=None,
                                    op0=ALU.logical_shift_right)
                e.drain()
                for h in range(2):
                    e.tensor_scalar(out=ibln[:, h:h + 1], in0=ibln[:, h:h + 1],
                                    scalar1=-1, scalar2=None,
                                    op0=ALU.bitwise_xor)
                e.drain()
                for h in range(2):
                    e.tensor_scalar(out=ibln[:, h:h + 1], in0=ibln[:, h:h + 1],
                                    scalar1=MAGIC + 1, scalar2=None, op0=ALU.add)
                e.drain()
                for h in range(2):
                    e.tensor_copy(stats[:, s + 12 + h:s + 13 + h],
                                  ibln[:, h:h + 1].bitcast(F32))
                e.drain()
                for _ in range(2):
                    for h in range(2):
                        e.scalar_tensor_tensor(
                            out=stats[:, s + 6 + h:s + 7 + h],
                            in0=stats[:, s + 12 + h:s + 13 + h],
                            scalar=stats[:, s + 10 + h:s + 11 + h],
                            in1=stats[:, s + 12 + h:s + 13 + h],
                            op0=ALU.mult, op1=ALU.mult)
                    e.drain()
                    for h in range(2):
                        e.tensor_scalar(out=stats[:, s + 6 + h:s + 7 + h],
                                        in0=stats[:, s + 6 + h:s + 7 + h],
                                        scalar1=-0.5, scalar2=None, op0=ALU.mult)
                    e.drain()
                    for h in range(2):
                        e.tensor_scalar(out=stats[:, s + 6 + h:s + 7 + h],
                                        in0=stats[:, s + 6 + h:s + 7 + h],
                                        scalar1=1.5, scalar2=None, op0=ALU.add)
                    e.drain()
                    for h in range(2):
                        e.tensor_scalar(out=stats[:, s + 12 + h:s + 13 + h],
                                        in0=stats[:, s + 12 + h:s + 13 + h],
                                        scalar1=stats[:, s + 6 + h:s + 7 + h],
                                        scalar2=None, op0=ALU.mult)
                    e.drain()
                last = None
                for h in range(2):
                    last = e.tensor_scalar(out=stats[:, s + 14 + h:s + 15 + h],
                                           in0=stats[:, s + 4 + h:s + 5 + h],
                                           scalar1=stats[:, s + 12 + h:s + 13 + h],
                                           scalar2=None, op0=ALU.mult)
                return last
            tok_rstd[j] = E("vector", rstdchain, waits=[t_var], inc="cv")

            t_xn = [None, None]
            for h in range(2):
                t_xn[h] = E("scalar", lambda e, b=cur_x, h=h, s=sc: act(
                    e, xn_tm[:, h, :], b[:, h, :], ACT.Identity,
                    scale=stats[:, s + 12 + h:s + 13 + h],
                    bias=stats[:, s + 14 + h:s + 15 + h]),
                    waits=[tok_rstd[j],
                           tok_xnT[j - 1] if j >= 1 else None] if h == 0 else (),
                    inc="ca")

            return t_xn

        def emit_LN_Q(j):
            par = j % 2
            sc = 32 * par

            t_amax = None
            for h in range(2):
                t_amax = E("vector", lambda e, s=sc, h=h: e.tensor_reduce(
                    out=stats[:, s + 16 + h:s + 17 + h], in_=xn_tm[:, h, :],
                    axis=AX, op=ALU.max, apply_absolute_value=True),
                    waits=[tok_xn[j][1]] if h == 0 else (),
                    inc="cv" if h == 1 else None)

            def qshchain(e, s=sc):
                for h in range(2):
                    e.tensor_scalar(out=stats[:, s + 18 + h:s + 19 + h],
                                    in0=stats[:, s + 16 + h:s + 17 + h],
                                    scalar1=1e-12, scalar2=None, op0=ALU.max)
                e.drain()
                for h in range(2):
                    e.reciprocal(stats[:, s + 20 + h:s + 21 + h],
                                 stats[:, s + 18 + h:s + 19 + h])
                for h in range(2):
                    e.tensor_scalar(out=stats[:, s + 24 + h:s + 25 + h],
                                    in0=stats[:, s + 18 + h:s + 19 + h],
                                    scalar1=sm[:, SM_WS:SM_WS + 1],
                                    scalar2=2.0 / 448.0,
                                    op0=ALU.mult, op1=ALU.mult)
                e.drain()
                for h in range(2):
                    e.tensor_scalar(out=stats[:, s + 22 + h:s + 23 + h],
                                    in0=stats[:, s + 20 + h:s + 21 + h],
                                    scalar1=224.0, scalar2=None, op0=ALU.mult)
                return e.drain()
            t_qsh = E("vector", qshchain, waits=[t_amax], inc="cv")

            for h in range(2):
                E("scalar", lambda e, h=h, s=sc: act(
                    e, q8_tm[:, h, :], xn_tm[:, h, :], ACT.Copy,
                    scale=stats[:, s + 22 + h:s + 23 + h]),
                    waits=[t_qsh, tok_qT[j - 1] if j >= 1 else None,
                           tok_mix[j - 1] if j >= 1 else None]
                    if h == 0 else ())
            E("scalar", lambda e: e.drain())
            t_qs8 = [None, None]
            for h in range(2):
                t_qs8[h] = E("scalar", lambda e, h=h, s=sc: act(
                    e, qs_tm[:, h, :], q8_tm[:, h, :], ACT.Copy,
                    scale=stats[:, s + 24 + h:s + 25 + h]), inc="ca")
            tok_qs[j] = t_qs8[1]
            return t_qs8

        def emit_transposes(j, t_qs8):
            cur_f = xn_f[j % 2]
            ca_xn = ("ca", sch.counts["ca"])
            ev_xn = [None] * 4
            t_xnT_ct = None
            for r in range(4):
                h = r // 2
                clo = (r % 2) * 4
                bank = pXT if r % 2 == 0 else pW[0]
                bnk = bk_XT if r % 2 == 0 else bk_W[0]
                tw = [ca_xn, bnk.producer_wait(), t_idf if j == 0 else None]
                tt = None
                for i in range(4):
                    tt = E("tensor", lambda e, h=h, c=clo + i, b=bank, i=i: e.transpose(
                        b[:, 128 * i:128 * (i + 1)],
                        xn_tm[:, h, 128 * c:128 * (c + 1)], identf[:, :]),
                        waits=tw if i == 0 else (),
                        inc="ct" if i == 3 else None)
                t_xnT_ct = tt
                ev_xn[r] = E("scalar", lambda e, b=bank, h=h, clo=clo, f=cur_f:
                             act(e, f[:, clo:clo + 4, 128 * h:128 * (h + 1)],
                                 b[:, :].rearrange("p (a b) -> p a b", b=128),
                                 ACT.Copy),
                             waits=[tt], inc="ca")
                bnk.consumed(ev_xn[r])
            tok_xnT[j] = t_xnT_ct
            tok_evxn[j] = ev_xn[3]
            ev_q = [None, None]
            t_qT_ct = None
            for h in range(2):
                tw = [t_qs8[h], bk_QT.producer_wait(), t_idb if j == 0 else None]
                tt = None
                for c in range(C):
                    tt = E("tensor", lambda e, h=h, c=c: e.transpose(
                        pQT[:, 128 * c:128 * (c + 1)],
                        qs_tm[:, h, 128 * c:128 * (c + 1)], identb[:, :]),
                        waits=tw if c == 0 else (),
                        inc="ct" if c == C - 1 else None)
                t_qT_ct = tt
                ev_q[h] = E("scalar", lambda e, h=h, qq=q_f[j % 2]: act(
                    e, qq[:, :, 128 * h:128 * (h + 1)],
                    pQT[:, :].rearrange("p (a b) -> p a b", b=128), ACT.Copy),
                    waits=[tt, tok_ffn[j - 2] if (j >= 2 and h == 0) else None],
                    inc="ca")
                bk_QT.consumed(ev_q[h])
            tok_evq[j] = ev_q[1]
            tok_qT[j] = t_qT_ct

        tok_xn[0] = emit_LN(0)
        t_qs8_0 = emit_LN_Q(0)
        emit_transposes(0, t_qs8_0)

        for j in range(NT):
            par = j % 2
            hb = 512 * par
            hbo = 512 * (1 - par)
            cur_x = x_tm[j % 2]
            cur_f = xn_f[j % 2]
            prv_f = xn_f[(j + 1) % 2]
            if j + 1 < NT:
                emit_LN_A(j + 1)

            cur_c = xcar[j % 2]
            if j > 0:
                E("vector", lambda e, cc=cur_c, p=prv_f: (
                    e.tensor_copy(cc[:, :, 0:1], p[:, :, TT - 1:TT]),
                    e.drain())[-1], waits=[tok_evxn[j]])

            def dxnchain(e, f=cur_f, cc=cur_c):
                e.tensor_tensor(out=dxn[:, :, 1:TT], in0=f[:, :, 1:TT],
                                in1=f[:, :, 0:TT - 1], op=ALU.subtract)
                for c in range(C):
                    e.tensor_scalar(out=dxn[:, c, 0:1], in0=f[:, c, 0:1],
                                    scalar1=cc[:, c, :], scalar2=None,
                                    op0=ALU.subtract)
                return e.drain()
            t_dxn = E("vector", dxnchain, waits=[tok_evxn[j]], inc="cv")
            mix_toks = {}
            mix_buf = {"g": mixA, "r": mixB, "k": mixA, "v": mixB}
            mix_m = {"g": SM_MG1, "r": SM_MR1, "k": SM_MK1, "v": SM_MV1}
            proj_done = {}
            kevac_runs = [None] * 4
            sg_runs = [None] * 4
            for pi, pk in enumerate("grkv"):
                buf = mix_buf[pk]
                wbuf = proj_done.get({"k": "g", "v": "r"}.get(pk))
                if wbuf is None:
                    wbuf = prev_proj_done.get({"g": "k", "r": "v"}.get(pk))
                tmx = None
                for c in range(C):
                    tmx = E("vector", lambda e, c=c, buf=buf, f=cur_f, mcol=mix_m[pk]:
                            e.scalar_tensor_tensor(
                                out=buf[:, c, :], in0=dxn[:, c, :],
                                scalar=sm[:, mcol + c:mcol + c + 1],
                                in1=f[:, c, :], op0=ALU.mult, op1=ALU.add),
                            waits=([wbuf] + ([t_mneg] if j == 0 else []))
                            if c == 0 else (),
                            inc="cv" if c == C - 1 else None)
                mix_toks[pk] = tmx

                for r in range(4):
                    gri = pi * 4 + r
                    bank = pP[gri % 2]
                    bnk = bk_P[gri % 2]
                    tw = [mix_toks[pk], bnk.producer_wait(),
                          t_w[pk] if j == 0 else None]
                    tt = None
                    for oo in range(2):
                        o = 2 * r + oo
                        for c in range(C):
                            tt = E("tensor", lambda e, c=c, o=o, oo=oo, buf=buf,
                                   bank=bank, pk=pk: e.matmul(
                                       bank[:, 256 * oo:256 * (oo + 1)],
                                       W[pk][:, c, 128 * o:128 * (o + 1)],
                                       buf[:, c, :], start=(c == 0), stop=(c == C - 1)),
                                   waits=tw if (oo == 0 and c == 0) else (),
                                   inc="ct" if (oo == 1 and c == C - 1) else None)
                    proj_done[pk] = tt
                    if pk == "g":
                        tev = None
                        for oo in range(2):
                            c = 2 * r + oo
                            tev = E("scalar", lambda e, bank=bank, oo=oo, c=c:
                                    act(e, sgb[:, c, :],
                                        bank[:, 256 * oo:256 * (oo + 1)],
                                        ACT.Sigmoid,
                                        bias=sm[:, SM_BG + c:SM_BG + c + 1]),
                                    waits=[tt, tok_gated[j - 1] if j >= 1 and r == 0
                                           and oo == 0 else None],
                                    inc="ca" if oo == 1 else None)
                        bnk.consumed(tev)
                        sg_runs[r] = tev
                    elif pk == "r":
                        tev = None
                        for oo in range(2):
                            c = 2 * r + oo
                            tev = E("scalar", lambda e, bank=bank, oo=oo, c=c:
                                    act(e, rbuf[:, c, :],
                                        bank[:, 256 * oo:256 * (oo + 1)],
                                        ACT.Identity,
                                        bias=sm[:, SM_BR + c:SM_BR + c + 1]),
                                    waits=[tt] if oo == 0 else (),
                                    inc="ca" if oo == 1 else None)
                        bnk.consumed(tev)
                    elif pk == "k":
                        tev = None
                        for oo in range(2):
                            c = 2 * r + oo
                            tev = E("scalar", lambda e, bank=bank, oo=oo, c=c:
                                    act(e, kbuf[:, c, :],
                                        bank[:, 256 * oo:256 * (oo + 1)],
                                        ACT.Identity,
                                        bias=sm[:, SM_BK + c:SM_BK + c + 1]),
                                    waits=[tt] if oo == 0 else (),
                                    inc="ca" if oo == 1 else None)
                        bnk.consumed(tev)
                        kevac_runs[r] = tev
                    else:
                        tev = None
                        for oo in range(2):
                            c = 2 * r + oo
                            tev = E("vector", lambda e, bank=bank, oo=oo, c=c:
                                    e.scalar_tensor_tensor(
                                        out=kbuf[:, c, :],
                                        in0=bank[:, 256 * oo:256 * (oo + 1)],
                                        scalar=sm[:, SM_BV + c:SM_BV + c + 1],
                                        in1=kbuf[:, c, :], op0=ALU.add, op1=ALU.mult),
                                    waits=[tt, kevac_runs[r]] if oo == 0 else (),
                                    inc="cv" if oo == 1 else None)
                        bnk.consumed(tev)
                if pk == "r" and j + 1 < NT:
                    tok_xn[j + 1] = emit_LN(j + 1)
            prev_proj_done = dict(proj_done)
            tok_mix[j] = mix_toks["v"]
            if j + 1 < NT:
                t_qs8_n = emit_LN_Q(j + 1)

            for c in range(C):
                E("vector", lambda e, c=c: e.tensor_tensor_scan(
                    out=accb[:, c, :],
                    data0=decv[:, c:c + 1].broadcast_to([128, TT]),
                    data1=kbuf[:, c, :],
                    initial=carry[:, c, :], op0=ALU.mult, op1=ALU.add),
                    waits=[t_dec, t_blo if j == 0 else None,
                           tok_sqmm[j - 1] if j >= 1 else None]
                    if c == 0 else ())

            E("vector", lambda e: e.drain())

            t_g2m = [None] * 4
            slot_banks = [(pW[0], bk_W[0]), (pW[1], bk_W[1]), (pHB, bk_HB)]
            for s_i in range(4):
                h, kk = s_i // 2, s_i % 2
                bank, bnk = slot_banks[s_i % 3]
                tw = [tok_evq[j], bnk.producer_wait(),
                      t_w8cast if j == 0 else None, t_blo if j == 0 else None]
                tt = E("tensor", lambda e, bank=bank, kk=kk: e.matmul(
                    bank[:, :], ones2[:, :],
                    bias_ops[:, 1024 + 512 * kk:1024 + 512 * (kk + 1)],
                    start=True, stop=False), waits=tw)
                for c in range(C):
                    tt = E("tensor", lambda e, c=c, h=h, kk=kk, bank=bank,
                           qq=q_f[j % 2]: e.matmul(
                        bank[:, :], qq[:, c, 128 * h:128 * (h + 1)],
                        W8[:, c, 512 * kk:512 * (kk + 1)],
                        start=False, stop=(c == C - 1)),
                        inc="ct" if c == C - 1 else None)
                tok_ffn[j] = tt
                tmp_war = (t_g2m[s_i - 2] if s_i >= 2 else
                           (slot_resid_all[j - 1][s_i + 2] if j >= 1 else None))
                t_sig = E("scalar", lambda e, bank=bank, si=s_i: act(
                    e, tmp[:, si % 2, :], bank[:, :], ACT.Sigmoid),
                    waits=[tt, tmp_war], inc="ca")
                t_g2m[s_i] = E("vector", lambda e, bank=bank, si=s_i, h=h, kk=kk:
                               e.scalar_tensor_tensor(
                                   out=outb[:, h, 512 * kk:512 * (kk + 1)],
                                   in0=bank[:, :], scalar=1.0,
                                   in1=tmp[:, si % 2, :],
                                   op0=ALU.mult, op1=ALU.mult),
                               waits=[t_sig, tok_store[j - 1] if (j >= 1 and s_i == 0)
                                      else None],
                               inc="cv")
                bnk.consumed(t_g2m[s_i])

            if j + 1 < NT:
                emit_transposes(j + 1, t_qs8_n)

            for c in range(C):
                E("vector", lambda e, c=c: e.scalar_tensor_tensor(
                    out=kbuf[:, c, :], in0=kbuf[:, c, :],
                    scalar=sm[:, SM_U + c:SM_U + c + 1], in1=accb[:, c, :],
                    op0=ALU.mult, op1=ALU.add))
            t_rw = E("vector", lambda e: (
                e.drain(),
                e.tensor_tensor(out=rbuf[:, :, :], in0=rbuf[:, :, :],
                                in1=kbuf[:, :, :], op=ALU.mult))[-1], inc="cv")

            def carrycopy(e):
                e.drain()
                return e.tensor_copy(carry[:, :, 0:1], accb[:, :, TT - 1:TT])
            t_carry = E("vector", carrycopy, inc="cv")

            sqbf = (acc_flat.bitcast(BF16)[:, 0:2048]
                    .rearrange("p (c t) -> p c t", t=TT))
            t_sq0 = E("scalar", lambda e, s=sqbf: act(
                e, s[:, 0:4, :], rbuf[:, 0:4, :], ACT.Square),
                waits=[t_carry], inc="ca")
            t_sqrw = E("scalar", lambda e, s=sqbf: act(
                e, s[:, 4:8, :], rbuf[:, 4:8, :], ACT.Square), inc="ca")
            sq_fm = sqbf

            w_hs = [t_rw, bk_HS.producer_wait(), t_bo if j == 0 else None]
            tt = None
            for c in range(C):
                tt = E("tensor", lambda e, c=c: e.matmul(
                    pHS[:, 0:256], bones[:, 16 * c:16 * (c + 1)],
                    rbuf[:, c, :], start=(c == 0), stop=(c == C - 1)),
                    waits=w_hs if c == 0 else ())
            for c in range(C):
                tt = E("tensor", lambda e, c=c: e.matmul(
                    pHS[:, 256:512], bones[:, 16 * c:16 * (c + 1)],
                    sq_fm[:, c, :], start=(c == 0), stop=(c == C - 1)),
                    waits=[t_sq0] if c == 0 else
                    ([t_sqrw] if c == 4 else ()),
                    inc="ct" if c == C - 1 else None)
            t_hs = tt
            tok_sqmm[j] = t_hs

            t_hvar = E("vector", lambda e, hb=hb, hbo=hbo: (
                e.tensor_scalar(out=hbuf[:, hb:hb + 512], in0=pHS[:, 0:512],
                                scalar1=1.0 / HD, scalar2=None, op0=ALU.mult),
                e.drain(),
                e.tensor_tensor(out=hbuf[:, hbo:hbo + 256],
                                in0=hbuf[:, hb:hb + 256], in1=hbuf[:, hb:hb + 256],
                                op=ALU.mult),
                e.drain(),
                e.scalar_tensor_tensor(out=hbuf[:, hb + 256:hb + 512],
                                       in0=hbuf[:, hb + 256:hb + 512],
                                       scalar=1e-5,
                                       in1=hbuf[:, hbo:hbo + 256],
                                       op0=ALU.add, op1=ALU.subtract))[-1],
                waits=[t_hs], inc="cv")
            bk_HS.consumed(t_hvar)

            def hrsdchain(e, hb=hb, hbo=hbo):
                vx = hbuf[0:16, hb + 256:hb + 512]
                y = hbuf[0:16, hbo + 256:hbo + 512]
                p = hbuf[0:16, hbo:hbo + 256]
                e.tensor_scalar(out=ibh[:, :], in0=vx.bitcast(mybir.dt.int32),
                                scalar1=1, scalar2=-1,
                                op0=ALU.logical_shift_right,
                                op1=ALU.bitwise_xor)
                e.drain()
                e.tensor_scalar(out=ibh[:, :], in0=ibh[:, :], scalar1=MAGIC + 1,
                                scalar2=None, op0=ALU.add)
                e.drain()
                e.tensor_copy(y, ibh[:, :].bitcast(F32))
                e.drain()
                for _ in range(2):
                    e.tensor_tensor(out=p, in0=y, in1=y, op=ALU.mult)
                    e.drain()
                    e.tensor_tensor(out=p, in0=p, in1=vx, op=ALU.mult)
                    e.drain()
                    e.tensor_scalar(out=p, in0=p, scalar1=-0.5, scalar2=1.5,
                                    op0=ALU.mult, op1=ALU.add)
                    e.drain()
                    e.tensor_tensor(out=y, in0=y, in1=p, op=ALU.mult)
                    e.drain()
                e.drain()
                bst = ibh[:, :].bitcast(BF16)
                e.tensor_copy(bst[:, 0:256], hbuf[0:16, hb:hb + 256])
                return e.tensor_copy(bst[:, 256:512], y)
            t_hrsd = E("vector", hrsdchain, waits=[t_hvar], inc="cv")

            t_lnc = {}
            for c in range(C):
                bank = pHB if c % 2 == 0 else pW[1]
                bnk = bk_HB if c % 2 == 0 else bk_W[1]
                tw = [t_hrsd, bnk.producer_wait(), t_sel if j == 0 else None]
                tb = E("tensor", lambda e, c=c, bank=bank: e.matmul(
                    bank[:, 0:512], sel[:, 128 * c:128 * (c + 1)],
                    ibh[:, :].bitcast(BF16), start=True, stop=True),
                    waits=tw, inc="ct")
                w_g = [tb, sg_runs[c // 2]]
                tg = E("vector", lambda e, c=c, bank=bank: (
                    e.tensor_tensor(out=gat[:, c, :], in0=rbuf[:, c, :],
                                    in1=bank[:, 0:256], op=ALU.subtract),
                    e.drain(),
                    e.tensor_tensor(out=gat[:, c, :], in0=gat[:, c, :],
                                    in1=bank[:, 256:512], op=ALU.mult),
                    e.drain(),
                    e.tensor_tensor(out=gat[:, c, :], in0=sgb[:, c, :],
                                    in1=gat[:, c, :], op=ALU.mult))[-1],
                    waits=w_g, inc="cv")
                t_lnc[c] = tg
                bnk.consumed(tg)
            tok_gated[j] = t_lnc[C - 1]

            for s_i in range(4):
                h, kk = s_i // 2, s_i % 2
                bank, bnk = slot_banks[s_i % 3]
                tw2 = [tok_gated[j], bnk.producer_wait(),
                       t_w["o"] if j == 0 else None]
                tt = E("tensor", lambda e, bank=bank, kk=kk: e.matmul(
                    bank[:, :], ones2[:, :],
                    bias_ops[:, 512 * kk:512 * (kk + 1)],
                    start=True, stop=False), waits=tw2)
                for c in range(C):
                    tt = E("tensor", lambda e, c=c, h=h, kk=kk, bank=bank: e.matmul(
                        bank[:, :], gat[:, c, 128 * h:128 * (h + 1)],
                        W["o"][:, c, 512 * kk:512 * (kk + 1)],
                        start=False, stop=(c == C - 1)),
                        inc="ct" if c == C - 1 else None)
                if s_i == 3:
                    tok_wo[j] = tt
                t_res = E("vector", lambda e, bank=bank, h=h, kk=kk, si=s_i,
                          b=cur_x: (
                    e.tensor_tensor(out=tmp[:, si % 2, :], in0=bank[:, :],
                                    in1=outb[:, h, 512 * kk:512 * (kk + 1)],
                                    op=ALU.mult),
                    e.drain(),
                    e.tensor_tensor(out=outb[:, h, 512 * kk:512 * (kk + 1)],
                                    in0=tmp[:, si % 2, :],
                                    in1=b[:, h, 512 * kk:512 * (kk + 1)],
                                    op=ALU.add))[-1],
                    waits=[tt], inc="cv")
                bnk.consumed(t_res)
                slot_resid_all[j][s_i] = t_res
                if s_i == 3:
                    resid_done[j] = t_res

            dstv = out_h[j * TT:(j + 1) * TT, :].rearrange("(h p) d -> p h d", p=128)
            tok_store[j] = E("gpsimd", lambda e, d=dstv: e.dma_start(
                out=d, in_=outb[:, :, :]), waits=[resid_done[j]],
                inc="co", inc_n=16)

            if j + 2 < NT:
                emit_xload(j + 2)

        E("gpsimd", lambda e: None, waits=[("co", sch.counts["co"])])

        with nc.Block() as block:
            for engn, deco in [("sync", block.sync), ("gpsimd", block.gpsimd),
                               ("tensor", block.tensor), ("vector", block.vector),
                               ("scalar", block.scalar)]:
                def body(eng, engn=engn):
                    for (en, fn, waits, inc_sem, inc_n) in sch.ops:
                        if en != engn:
                            continue
                        for (semname, val) in waits:
                            eng.wait_ge(sch.sems[semname], val)
                        inst = fn(eng)
                        if inc_n:
                            if engn in ("vector", "scalar"):
                                inst = eng.drain()
                            elif inst is None:
                                inst = eng.nop()
                            inst.then_inc(inc_sem, inc_n)
                deco(body)
    return nc


def _host_prep(inputs):
    f32 = np.float32
    x = np.ascontiguousarray(inputs["L0_attn_x"])
    per = {}
    per["wgT"] = np.ascontiguousarray(inputs["L0_attn_Wg_w"].T)
    per["wrT"] = np.ascontiguousarray(inputs["L0_attn_Wr_w"].T)
    per["wkT"] = np.ascontiguousarray(inputs["L0_attn_Wk_w"].T)
    per["wvT"] = np.ascontiguousarray(inputs["L0_attn_Wv_w"].T)
    per["woT"] = np.ascontiguousarray(inputs["L0_attn_Wo_w"].T)
    per["w8T"] = np.ascontiguousarray(
        inputs["L0_ffn_weight"].T.astype(np.float32))

    def cols(v):
        return np.ascontiguousarray(np.asarray(v, f32).reshape(C, 128).T)

    sm = np.zeros((128, SM_COLS), f32)
    sm[:, SM_MG:SM_MG + 8] = cols(inputs["L0_attn_mg"])
    sm[:, SM_MR:SM_MR + 8] = cols(inputs["L0_attn_mr"])
    sm[:, SM_MK:SM_MK + 8] = cols(inputs["L0_attn_mk"])
    sm[:, SM_MV:SM_MV + 8] = cols(inputs["L0_attn_mv"])
    sm[:, SM_BG:SM_BG + 8] = cols(inputs["L0_attn_Wg_b"])
    sm[:, SM_BR:SM_BR + 8] = cols(inputs["L0_attn_Wr_b"])
    sm[:, SM_BK:SM_BK + 8] = cols(inputs["L0_attn_Wk_b"])
    sm[:, SM_BV:SM_BV + 8] = cols(inputs["L0_attn_Wv_b"])
    sm[:, SM_W:SM_W + 8] = cols(np.asarray(inputs["L0_attn_w"], f32).reshape(-1))
    sm[:, SM_U:SM_U + 8] = cols(np.asarray(inputs["L0_attn_u"], f32).reshape(-1))
    sm[:, SM_WS] = np.float32(inputs["L0_ffn_weight_scale"])
    sm[:, SM_ONE] = 1.0
    per["smalls"] = sm

    pack2 = np.zeros((1, 2048), f32)
    pack2[0, 0:1024] = np.asarray(inputs["L0_attn_Wo_b"], f32)
    pack2[0, 1024:2048] = np.asarray(inputs["L0_ffn_bias"], f32)
    per["pack2"] = pack2

    per["identf"] = np.eye(128, dtype=f32)
    per["identb"] = np.eye(128).astype(ml_dtypes.bfloat16)
    bones = np.zeros((128, 128), f32)
    for c in range(C):
        for p in range(128):
            bones[p, 16 * c + (2 * c + p // 64)] = 1.0
    per["bones"] = bones.astype(ml_dtypes.bfloat16)
    sel = np.zeros((16, 1024), f32)
    for c in range(C):
        for m in range(128):
            sel[2 * c + m // 64, 128 * c + m] = 1.0
    per["sel"] = sel.astype(ml_dtypes.bfloat16)
    return x, per


def kernel(**inputs):
    x, per = _host_prep(inputs)
    B, N, _ = x.shape
    NT = N // TT
    nc = build(NT=NT)
    in_maps = []
    for b in range(B):
        m = {"x": np.ascontiguousarray(x[b]),
             "xlast": np.ascontiguousarray(x[b, N - 1:N, :])}
        m.update(per)
        in_maps.append(m)
    res = run_bass_kernel_spmd(nc, in_maps, core_ids=list(range(B)))
    out = np.stack([res.results[b]["out"] for b in range(B)], axis=0)
    return out.astype(np.float32)
